# version 17
# baseline (speedup 1.0000x reference)
import numpy as np
import ml_dtypes

import concourse.bass as bass
import concourse.mybir as mybir
from concourse.bass_utils import run_bass_kernel_spmd
from contextlib import ExitStack

F32 = mybir.dt.float32
BF16 = mybir.dt.bfloat16
AX = mybir.AxisListType.X
ALU = mybir.AluOpType
ACT = mybir.ActivationFunctionType
BF = ml_dtypes.bfloat16

NCORE = 8
D = 1024
T = 2048
H = 16
FF = 4096
NC = D // 128
NQG = T // 512
NM1 = FF // 128
EPS = 1e-5
MAGIC = 12582912.0
NEG = -1.0e30

N1 = NCORE * T * D
N2 = NCORE * T * FF
R1 = 0.005 * (N1 / 8.0)
R2 = 0.005 * (N2 / 32.0)
NBOOT = 6
NFINE = 10


def _build():
    nc = bass.Bass("TRN2", target_bir_lowering=False)

    x_d = nc.dram_tensor("x_t", [D, T], F32, kind="ExternalInput")
    wqkv_d = nc.dram_tensor("wqkvT", [D, 3 * D], F32, kind="ExternalInput")
    wo_d = nc.dram_tensor("woT", [D, D], F32, kind="ExternalInput")
    w1_d = nc.dram_tensor("w1T", [D, FF], F32, kind="ExternalInput")
    w2_d = nc.dram_tensor("w2T", [FF, D], F32, kind="ExternalInput")
    qkvb_d = nc.dram_tensor("qkvb", [128, 16], F32, kind="ExternalInput")
    vbias_d = nc.dram_tensor("vbiast", [128, D], F32, kind="ExternalInput")
    wob_d = nc.dram_tensor("wob", [128, NC], F32, kind="ExternalInput")
    lng_d = nc.dram_tensor("lng", [128, NC], F32, kind="ExternalInput")
    lnb_d = nc.dram_tensor("lnb", [128, NC], F32, kind="ExternalInput")
    ghost_d = nc.dram_tensor("ghostt", [128, H], F32, kind="ExternalInput")
    ident_d = nc.dram_tensor("identb", [128, 128], BF16, kind="ExternalInput")
    mask_d = nc.dram_tensor("maskb", [128, 128], BF16, kind="ExternalInput")
    bootf_d = nc.dram_tensor("bootf", [1, NBOOT], F32, kind="ExternalInput")
    finef_d = nc.dram_tensor("finef", [1, NFINE], F32, kind="ExternalInput")
    out_d = nc.dram_tensor("out_t", [D, T], F32, kind="ExternalOutput")

    h1_d = nc.dram_tensor("h1s", [D, T], F32, kind="Internal")
    x2_d = nc.dram_tensor("x2s", [D, T], F32, kind="Internal")
    h2_d = nc.dram_tensor("h2s", [FF, T], F32, kind="Internal")
    t1_d = nc.dram_tensor("t1s", [D, FF], BF16, kind="Internal")
    t2_d = nc.dram_tensor("t2s", [FF, D], BF16, kind="Internal")
    amx1_d = nc.dram_tensor("amx1", [128, 1], F32, kind="Internal")
    amx2_d = nc.dram_tensor("amx2", [128, 1], F32, kind="Internal")
    h1ss_d = nc.dram_tensor("h1ss", [128, 2048], F32, kind="Internal")
    ccd = {}
    for tag, n_in in [("a1", 2), ("b1", NBOOT), ("c1", NFINE),
                      ("a2", 2), ("b2", NBOOT), ("c2", NFINE)]:
        ccd[tag] = (nc.dram_tensor("cci_" + tag, [1, n_in], F32, kind="Internal"),
                    nc.dram_tensor("cco_" + tag, [1, n_in * NCORE], F32, kind="Internal"))

    EM = {"sync": [], "gpsimd": [], "tensor": [], "vector": [], "scalar": []}
    C = {}
    SEM = {}

    def emit(eng, fn):
        EM[eng].append(fn)

    ESEM = {"vector": "tv", "scalar": "ts"}
    DMAQ = {"lx"}

    def inc(eng, mk, semname, d, prewait=True):
        if prewait:
            s = ESEM.get(eng)
            if s is not None and C[s] > 0:
                sem0 = SEM[s]
                v0 = C[s]
                emit(eng, lambda e, sem0=sem0, v0=v0: e.wait_ge(sem0, v0))
        if semname in DMAQ and C[semname] > 0:
            semq = SEM[semname]
            vq = C[semname]
            emit(eng, lambda e, semq=semq, vq=vq: e.wait_ge(semq, vq))
        C[semname] += d
        sem = SEM[semname]
        emit(eng, lambda e, mk=mk, sem=sem, d=d: mk(e).then_inc(sem, d))
        return C[semname]

    def raw(eng, mk):
        s = ESEM.get(eng)
        if s is None:
            emit(eng, lambda e, mk=mk: mk(e))
            return
        wait(eng, s, C[s])
        inc(eng, mk, s, 1)

    def wait(eng, semname, val):
        if val <= 0:
            return
        sem = SEM[semname]
        emit(eng, lambda e, sem=sem, val=val: e.wait_ge(sem, val))

    def drain(eng):
        pass

    stack = ExitStack()
    with stack:
        E = stack.enter_context
        for s in ["lx", "lg", "cc", "tv", "ts", "tt", "so"]:
            SEM[s] = E(nc.semaphore(s))
            C[s] = 0

        qf = E(nc.sbuf_tensor("qf", [128, NC * T], BF16))
        kf = E(nc.sbuf_tensor("kf", [128, NC * T], BF16))
        vt = E(nc.sbuf_tensor("vt", [128, NC * T], BF16))
        qkvb = E(nc.sbuf_tensor("qkvbs", [128, 16], F32))
        vbias = E(nc.sbuf_tensor("vbiass", [128, D], BF16))
        wob = E(nc.sbuf_tensor("wobs", [128, NC], F32))
        lng = E(nc.sbuf_tensor("lngs", [128, NC], F32))
        lnb = E(nc.sbuf_tensor("lnbs", [128, NC], F32))
        ghost = E(nc.sbuf_tensor("ghosts", [128, H], F32))
        identb = E(nc.sbuf_tensor("identbs", [128, 128], BF16))
        maskb = E(nc.sbuf_tensor("maskbs", [128, 128], BF16))
        onesb = E(nc.sbuf_tensor("onesb", [128, 128], BF16))
        onesf = E(nc.sbuf_tensor("onesf", [128, 128], F32))
        onesrow = E(nc.sbuf_tensor("onesrow", [1, 128], F32))
        bootf = E(nc.sbuf_tensor("bootfs", [1, NBOOT], F32))
        finef = E(nc.sbuf_tensor("finefs", [1, NFINE], F32))
        cols = E(nc.sbuf_tensor("colsa", [128, 32], F32))
        rows = E(nc.sbuf_tensor("rowsa", [1, 128], F32))
        cnt1 = E(nc.sbuf_tensor("cnt1", [128, NFINE], F32))
        callc = E(nc.sbuf_tensor("callc", [128, 16 * NFINE], F32))
        cnt2 = cnt1
        bcnt = E(nc.sbuf_tensor("bcnt", [128, NBOOT], F32))
        thrb = E(nc.sbuf_tensor("thrb", [128, NBOOT], F32))
        thrf = E(nc.sbuf_tensor("thrf", [128, NFINE], F32))
        svals = E(nc.sbuf_tensor("svals", [128, 8], F32))
        grow = E(nc.sbuf_tensor("grow", [1, 32], F32))
        epscol = E(nc.sbuf_tensor("epscol", [128, 1], F32))

        ELL, EMX, RR, RINV = 0, 2, 4, 6
        CTMP = 20
        SV_S1, SV_GS, SV_S2, SV_EPS5 = 0, 1, 2, 3

        pbS = E(nc.psum_tensor("pbS", [128, 2048], F32))
        pbT = E(nc.psum_tensor("pbT", [128, 2048], BF16))
        pbA = E(nc.psum_tensor("pbA", [128, 512], F32))
        pbB = E(nc.psum_tensor("pbB", [128, 512], F32))
        pbY = pbB

        ln1x = E(nc.sbuf_tensor("ln1x", [128, NC * T], BF16))
        n1 = ln1x
        xg = E(nc.sbuf_tensor("xg", [128, 4096], F32))
        wgrp = E(nc.sbuf_tensor("wgrp", [128, 4096], BF16))
        pbuf = ln1x[:, 0:4096]
        ptb = ln1x[:, 4096:6144]
        asb = ln1x[:, 6144:10240]
        ysb = ln1x[:, 10240:14336]
        bscr = ln1x[:, 14336:15360]
        h1sb = E(nc.sbuf_tensor("h1sb", [128, 4096], F32))
        scrA = E(nc.sbuf_tensor("scrA", [128, 4096], F32))
        wstr = E(nc.sbuf_tensor("wstr", [128, 2048], F32))
        srep = E(nc.sbuf_tensor("srep", [128, 512 * 5], F32))

        def sr(i):
            if i == 5:
                return scrA[:, 3584:4096]
            return srep[:, 512 * i:512 * i + 512]

        inc("gpsimd", lambda e: e.dma_start(out=vbias[:], in_=vbias_d.ap()), "lg", 16)
        for dst, src in [(qkvb, qkvb_d), (wob, wob_d),
                         (lng, lng_d), (lnb, lnb_d), (ghost, ghost_d),
                         (identb, ident_d), (maskb, mask_d),
                         (bootf, bootf_d), (finef, finef_d)]:
            inc("sync", lambda e, dst=dst, src=src: e.dma_start(out=dst[:], in_=src.ap()), "lx", 16)
        for t_, v_ in [(onesb, 1.0), (onesf, 1.0), (onesrow, 1.0),
                       (cnt1, 0.0), (bcnt, 0.0), (epscol, EPS)]:
            raw("vector", lambda e, t_=t_, v_=v_: e.memset(t_[:], v_))
        raw("vector", lambda e: e.memset(cols[:], 0.0))
        inc("vector", lambda e: e.nop(), "tv", 1)

        for sweep in range(2):
            for c in range(NC):
                if sweep > 0 or c > 0:
                    wait("sync", "tv", C["tv"])
                lxv = inc("sync", lambda e, c=c: e.dma_start(
                    out=xg[:, 0:2048], in_=x_d.ap()[128 * c:128 * c + 128, :]), "lx", 16)
                wait("vector", "lx", lxv)
                if c > 0 or sweep > 0:
                    wait("vector", "tt", C["tt"])
                if sweep == 0:
                    tvv = inc("vector", lambda e: e.tensor_copy(
                        out=pbuf[:, 0:2048], in_=xg[:, 0:2048]), "tv", 1)
                    roff = 0
                else:
                    raw("vector", lambda e: e.tensor_copy(out=pbuf[:, 0:2048], in_=xg[:, 0:2048]))
                    tvv = inc("vector", lambda e: e.tensor_tensor(
                        out=pbuf[:, 2048:4096], in0=pbuf[:, 0:2048],
                        in1=pbuf[:, 0:2048], op=ALU.mult), "tv", 1)
                    roff = 2048
                wait("tensor", "tv", tvv)
                for qg in range(NQG):
                    inc("tensor", lambda e, c=c, qg=qg, roff=roff: e.matmul(
                        pbS[:, 512 * qg:512 * qg + 512], onesb[:],
                        pbuf[:, roff + 512 * qg:roff + 512 * qg + 512],
                        start=(c == 0), stop=(c == NC - 1)), "tt", 1)
            wait("vector", "tt", C["tt"])
            if sweep == 0:
                tvv = inc("vector", lambda e: e.tensor_scalar(
                    out=scrA[:, 0:2048], in0=pbS[:, 0:2048], scalar1=1.0 / D,
                    scalar2=None, op0=ALU.mult), "tv", 1)
                wait("tensor", "tv", tvv)
            else:
                raw("vector", lambda e: e.tensor_scalar(
                    out=scrA[:, 2048:4096], in0=pbS[:, 0:2048], scalar1=1.0 / D,
                    scalar2=None, op0=ALU.mult))
                drain("vector")
                raw("vector", lambda e: e.tensor_tensor(
                    out=wstr[:, 0:2048], in0=scrA[:, 0:2048], in1=scrA[:, 0:2048],
                    op=ALU.mult))
                drain("vector")
                tvv = inc("vector", lambda e: e.tensor_tensor(
                    out=scrA[:, 2048:4096], in0=scrA[:, 2048:4096],
                    in1=wstr[:, 0:2048], op=ALU.subtract), "tv", 1)
                wait("scalar", "tv", tvv)
                raw("scalar", lambda e: e.activation(
                    out=wstr[:, 0:2048], in_=scrA[:, 2048:4096], func=ACT.Ln, bias=epscol[:, 0:1]))
                tsv = inc("scalar", lambda e: e.activation(
                    out=wstr[:, 0:2048], in_=wstr[:, 0:2048], func=ACT.Exp,
                    scale=-0.5), "ts", 1)
                wait("vector", "ts", tsv)
                tvv = inc("vector", lambda e: e.scalar_tensor_tensor(
                    out=scrA[:, 0:2048], in0=scrA[:, 0:2048], scalar=-1.0,
                    in1=wstr[:, 0:2048], op0=ALU.mult, op1=ALU.mult), "tv", 1)
        for c in range(NC):
            wait("sync", "tv", C["tv"])
            lxv = inc("sync", lambda e, c=c: e.dma_start(
                out=xg[:, 0:2048], in_=x_d.ap()[128 * c:128 * c + 128, :]), "lx", 16)
            wait("vector", "lx", lxv)
            raw("vector", lambda e: e.tensor_tensor(
                out=xg[:, 2048:4096], in0=xg[:, 0:2048], in1=wstr[:, 0:2048], op=ALU.mult))
            drain("vector")
            tvv = inc("vector", lambda e, c=c: e.tensor_tensor(
                out=ln1x[:, 2048 * c:2048 * c + 2048], in0=xg[:, 2048:4096],
                in1=scrA[:, 0:2048], op=ALU.add), "tv", 1)
        tv_ln1 = C["tv"]

        wait("tensor", "tv", tv_ln1)

        for grp in range(4):
            m0 = 4 * grp
            wait("gpsimd", "tt", C["tt"])
            lgv = 0
            for k in range(NC):
                lgv = inc("gpsimd", lambda e, k=k, m0=m0: e.dma_start(
                    out=wgrp[:, 512 * k:512 * k + 512],
                    in_=wqkv_d.ap()[128 * k:128 * k + 128, 128 * m0:128 * m0 + 512]),
                    "lg", 16)
            wait("tensor", "lg", lgv)
            for qg in range(NQG):
                wait("tensor", "tv", C["tv"])
                for mm in range(4):
                    for k in range(NC):
                        inc("tensor", lambda e, qg=qg, mm=mm, k=k: e.matmul(
                            pbS[:, 512 * mm:512 * mm + 512],
                            wgrp[:, 512 * k + 128 * mm:512 * k + 128 * mm + 128],
                            ln1x[:, 2048 * k + 512 * qg:2048 * k + 512 * qg + 512],
                            start=(k == 0), stop=(k == NC - 1)), "tt", 1)
                ttv = C["tt"]
                wait("vector", "tt", ttv)
                for mm in range(4):
                    m = m0 + mm
                    is_q = m < 8
                    dst = qf if is_q else kf
                    mc = m if is_q else m - 8
                    inc("vector", lambda e, mm=mm, m=m, dst=dst, mc=mc, qg=qg,
                        is_q=is_q: e.tensor_scalar(
                            out=dst[:, 2048 * mc + 512 * qg:2048 * mc + 512 * qg + 512],
                            in0=pbS[:, 512 * mm:512 * mm + 512],
                            scalar1=qkvb[:, m:m + 1],
                            scalar2=(0.125 if is_q else None),
                            op0=ALU.add, op1=(ALU.mult if is_q else ALU.bypass)),
                        "tv", 1)
        for vg in range(2):
            wait("gpsimd", "tt", C["tt"])
            lgv = 0
            for k in range(NC):
                lgv = inc("gpsimd", lambda e, k=k, vg=vg: e.dma_start(
                    out=wgrp[:, 512 * k:512 * k + 512],
                    in_=wqkv_d.ap()[128 * k:128 * k + 128,
                                    2048 + 512 * vg:2048 + 512 * vg + 512]), "lg", 16)
            wait("tensor", "lg", lgv)
            if vg == 0:
                wait("vector", "lg", lgv)
            for tt_i in range(16):
                wait("tensor", "tv", C["tv"])
                for k in range(NC):
                    inc("tensor", lambda e, tt_i=tt_i, k=k: e.matmul(
                        pbS[:, 512 * (tt_i % 4):512 * (tt_i % 4) + 512],
                        ln1x[:, 2048 * k + 128 * tt_i:2048 * k + 128 * tt_i + 128],
                        wgrp[:, 512 * k:512 * k + 512],
                        start=(k == 0), stop=(k == NC - 1)), "tt", 1)
                ttv = C["tt"]
                wait("vector", "tt", ttv)
                raw("vector", lambda e, tt_i=tt_i, vg=vg: e.tensor_copy(
                    out=vt[:, 1024 * tt_i + 512 * vg:1024 * tt_i + 512 * vg + 512],
                    in_=pbS[:, 512 * (tt_i % 4):512 * (tt_i % 4) + 512]))
                drain("vector")
                inc("vector", lambda e, tt_i=tt_i, vg=vg: e.tensor_tensor(
                    out=vt[:, 1024 * tt_i + 512 * vg:1024 * tt_i + 512 * vg + 512],
                    in0=vt[:, 1024 * tt_i + 512 * vg:1024 * tt_i + 512 * vg + 512],
                    in1=vbias[:, 512 * vg:512 * vg + 512], op=ALU.add), "tv", 1)
        tv_qkv = C["tv"]
        wait("tensor", "tv", tv_qkv)

        tern_jobs = []
        for ch in range(NC):
            for half in range(2):
                tern_jobs.append(("g1", ch, half))
        for ch in range(0, 32, 2):
            tern_jobs.append(("g2", ch, 0))
        for ch in range(NC):
            for half in range(2):
                tern_jobs.append(("t1", ch, half))
        for ch in range(0, 32, 2):
            tern_jobs.append(("t2", ch, 0))
        tern_i = [0]
        tern_lx = [0]

        def w2row_ap(ch):
            return w2_d.ap()[128 * ch:128 * ch + 256, :].rearrange(
                "(a p) q -> p a q", p=128)

        def t2row_ap(ch):
            return t2_d.ap()[128 * ch:128 * ch + 256, :].rearrange(
                "(a p) q -> p a q", p=128)

        def v3d(ap2d):
            return ap2d.rearrange("p (a q) -> p a q", a=2)

        def emit_tern_some(n):
            for _ in range(n):
                if tern_i[0] >= len(tern_jobs):
                    return
                kind, ch, half = tern_jobs[tern_i[0]]
                tern_i[0] += 1
                wait("sync", "tv", C["tv"])
                if kind in ("g1", "t1"):
                    lxv = inc("sync", lambda e, ch=ch, half=half: e.dma_start(
                        out=wstr[:, 0:2048],
                        in_=w1_d.ap()[128 * ch:128 * ch + 128,
                                      2048 * half:2048 * half + 2048]), "lx", 16)
                else:
                    lxv = inc("sync", lambda e, ch=ch: e.dma_start(
                        out=v3d(wstr[:, 0:2048]), in_=w2row_ap(ch)), "lx", 16)
                if kind in ("g1", "g2"):
                    gc = 8 if kind == "g1" else 9
                    wait("scalar", "lx", lxv)
                    wait("scalar", "tv", C["tv"])
                    tsv = inc("scalar", lambda e: e.activation(
                        out=scrA[:, 0:2048], in_=wstr[:, 0:2048], func=ACT.Abs),
                        "ts", 1)
                    wait("vector", "ts", tsv)
                    raw("vector", lambda e: e.reduce_sum(
                        out=cols[:, CTMP:CTMP + 1], in_=scrA[:, 0:2048], axis=AX))
                    drain("vector")
                    tvv = inc("vector", lambda e, gc=gc: e.tensor_tensor(
                        out=cols[:, gc:gc + 1], in0=cols[:, gc:gc + 1],
                        in1=cols[:, CTMP:CTMP + 1], op=ALU.add), "tv", 1)
                    drain("vector")
                else:
                    igc = 10 if kind == "t1" else 11
                    wait("vector", "lx", lxv)
                    raw("vector", lambda e, igc=igc: e.tensor_scalar(
                        out=scrA[:, 0:2048], in0=wstr[:, 0:2048],
                        scalar1=cols[:, igc:igc + 1], scalar2=MAGIC,
                        op0=ALU.mult, op1=ALU.add))
                    drain("vector")
                    raw("vector", lambda e: e.tensor_scalar(
                        out=scrA[:, 0:2048], in0=scrA[:, 0:2048], scalar1=MAGIC,
                        scalar2=1.0, op0=ALU.subtract, op1=ALU.min))
                    drain("vector")
                    tvv = inc("vector", lambda e: e.tensor_scalar(
                        out=ptb[:, 0:2048], in0=scrA[:, 0:2048], scalar1=-1.0,
                        scalar2=None, op0=ALU.max), "tv", 1)
                    wait("sync", "tv", tvv)
                    if kind == "t1":
                        tern_lx[0] = inc("sync", lambda e, ch=ch, half=half: e.dma_start(
                            out=t1_d.ap()[128 * ch:128 * ch + 128,
                                          2048 * half:2048 * half + 2048],
                            in_=ptb[:, 0:2048]), "lx", 16)
                    else:
                        tern_lx[0] = inc("sync", lambda e, ch=ch: e.dma_start(
                            out=t2row_ap(ch), in_=v3d(ptb[:, 0:2048])), "lx", 16)

        def emit_gamma_finalize():
            wait("tensor", "tv", C["tv"])
            ttv = inc("tensor", lambda e: e.matmul(
                pbB[:, 0:2], onesf[:], cols[:, 8:10], start=True, stop=True), "tt", 1)
            wait("vector", "tt", ttv)
            for j in range(2):
                raw("vector", lambda e, j=j: e.tensor_scalar(
                    out=cols[:, 12 + j:13 + j], in0=pbB[:, j:j + 1],
                    scalar1=1.0 / (D * FF), scalar2=EPS, op0=ALU.mult, op1=ALU.add))
                drain("vector")
                raw("vector", lambda e, j=j: e.reciprocal(
                    out=cols[:, 10 + j:11 + j], in_=cols[:, 12 + j:13 + j]))
                drain("vector")
                raw("vector", lambda e, j=j: e.tensor_copy(
                    out=cols[:, 14 + j:15 + j], in_=cols[:, 12 + j:13 + j]))
                drain("vector")
            inc("vector", lambda e: e.nop(), "tv", 1)

        def stats_mm(rhs_tile, tvv_wait):
            wait("tensor", "tv", tvv_wait)
            for c in range(NC):
                inc("tensor", lambda e, c=c, rhs_tile=rhs_tile: e.matmul(
                    pbB[:, 0:512], onesb[:],
                    rhs_tile[:, 512 * c:512 * c + 512],
                    start=(c == 0), stop=(c == NC - 1)), "tt", 1)
            return C["tt"]

        def stats_sq_mm(src_ap, src_is_f32):
            for c in range(NC):
                sl = 512 * (c % 2)
                if c >= 2:
                    wait("vector", "tt", C["tt"])
                raw("vector", lambda e, c=c, sl=sl, src_ap=src_ap: e.tensor_tensor(
                    out=bscr[:, sl:sl + 512],
                    in0=src_ap[:, 512 * c:512 * c + 512],
                    in1=src_ap[:, 512 * c:512 * c + 512], op=ALU.mult))
                wait("tensor", "tv", C["tv"])
                inc("tensor", lambda e, c=c, sl=sl: e.matmul(
                    pbB[:, 0:512], onesb[:], bscr[:, sl:sl + 512],
                    start=(c == 0), stop=(c == NC - 1)), "tt", 1)
            return C["tt"]

        def stats_copy_mm(src_f32_ap):
            for c in range(NC):
                sl = 512 * (c % 2)
                if c >= 2:
                    wait("vector", "tt", C["tt"])
                raw("vector", lambda e, c=c, sl=sl, src_f32_ap=src_f32_ap:
                    e.tensor_copy(out=bscr[:, sl:sl + 512],
                                  in_=src_f32_ap[:, 512 * c:512 * c + 512]))
                wait("tensor", "tv", C["tv"])
                inc("tensor", lambda e, c=c, sl=sl: e.matmul(
                    pbB[:, 0:512], onesb[:], bscr[:, sl:sl + 512],
                    start=(c == 0), stop=(c == NC - 1)), "tt", 1)
            return C["tt"]

        tpP = [0, 0]
        sc_free = [0, 0, 0, 0]
        q1ss_lx = [0]

        for qg in range(NQG):
            wait("sync", "tv", C["tv"])
            lx_x = inc("sync", lambda e, qg=qg: e.dma_start(
                out=xg[:].rearrange("p (c q) -> p c q", c=NC),
                in_=x_d.ap().rearrange("(c p) q -> p c q", p=128)[
                    :, :, 512 * qg:512 * qg + 512]), "lx", 16)

            if tern_lx[0]:
                wait("vector", "lx", tern_lx[0])
            iters = [(h, qi) for h in range(H) for qi in range(4)]
            exp_of = {}
            pn_tv = {}

            def emit_scores_exp(it, qg=qg, iters=iters, exp_of=exp_of):
                h, qi = iters[it]
                ip = it % 2
                hb = 64 * (h % 2)
                hc = h // 2
                qt = 4 * qg + qi
                kext = 128 * (qt + 1)
                widths = [512] * (kext // 512)
                if kext % 512:
                    widths.append(kext % 512)
                piece_ts = [0] * len(widths)
                pend = []
                tsv = 0
                for p, w in enumerate(widths):
                    bk = 2 * ip + (p % 2)
                    if p >= 2:
                        wait("tensor", "ts", piece_ts[p - 2])
                    elif sc_free[bk]:
                        wait("tensor", "ts", sc_free[bk])
                    last = (p == len(widths) - 1)
                    inc("tensor", lambda e, hb=hb, hc=hc, qt=qt, p=p, w=w, bk=bk,
                        last=last: e.matmul(
                        pbS[:, 512 * bk:512 * bk + w],
                        qf[hb:hb + 64, 2048 * hc + 128 * qt:2048 * hc + 128 * qt + 128],
                        kf[hb:hb + 64, 2048 * hc + 512 * p:2048 * hc + 512 * p + w],
                        start=True, stop=not last), "tt", 1)
                    if last:
                        inc("tensor", lambda e, bk=bk, w=w: e.matmul(
                            pbS[:, 512 * bk + w - 128:512 * bk + w], identb[:],
                            maskb[:], start=False, stop=True), "tt", 1)
                    pend.append((p, w))
                    if p % 2 == 1 or last:
                        ttv = C["tt"]
                        wait("scalar", "tt", ttv)
                        j = pend[0][0] // 2
                        if j == 0:
                            wait("scalar", "tt", tpP[ip])
                        we = sum(w_ for _, w_ in pend)
                        acc_col = (ELL if j == 0 else RR) + ip
                        tsv = inc("scalar", lambda e, ip=ip, j=j, we=we,
                                  acc_col=acc_col: e.activation(
                            out=pbuf[:, 2048 * ip + 1024 * j:
                                     2048 * ip + 1024 * j + we],
                            in_=pbS[:, 1024 * ip:1024 * ip + we], func=ACT.Exp,
                            accum_out=cols[:, acc_col:acc_col + 1]), "ts", 1)
                        if j > 0:
                            tsv = inc("scalar", lambda e, ip=ip: e.add(
                                out=cols[:, ELL + ip:ELL + ip + 1],
                                in_=cols[:, ELL + ip:ELL + ip + 1],
                                add=cols[:, RR + ip:RR + ip + 1]), "ts", 1)
                        for p_, _ in pend:
                            piece_ts[p_] = tsv
                            sc_free[2 * ip + (p_ % 2)] = tsv
                        pend = []
                exp_of[it] = tsv

            def emit_vec(it, qg=qg, iters=iters, exp_of=exp_of, pn_tv=pn_tv):
                h, qi = iters[it]
                ip = it % 2
                qt = 4 * qg + qi
                kext = 128 * (qt + 1)
                wait("vector", "ts", exp_of[it])
                raw("vector", lambda e, ip=ip, kext=kext: e.reduce_max(
                    out=cols[:, EMX + ip:EMX + ip + 1],
                    in_=pbuf[:, 2048 * ip:2048 * ip + kext], axis=AX))
                if ip == 1:
                    raw("vector", lambda e, h=h: e.scalar_tensor_tensor(
                        out=cols[:, RR:RR + 2],
                        in0=cols[:, EMX:EMX + 2],
                        scalar=ghost[:, h:h + 1],
                        in1=cols[:, ELL:ELL + 2],
                        op0=ALU.mult, op1=ALU.add))
                    raw("vector", lambda e: e.reciprocal(
                        out=cols[:, RINV:RINV + 2],
                        in_=cols[:, RR:RR + 2]))
                    kxe = kext - 128
                    raw("vector", lambda e, kxe=kxe: e.tensor_scalar(
                        out=pbuf[:, 0:kxe],
                        in0=pbuf[:, 0:kxe],
                        scalar1=cols[:, RINV:RINV + 1], scalar2=None,
                        op0=ALU.mult))
                    pn_tv[it - 1] = C["tv"]
                    pn_tv[it] = inc("vector", lambda e, kext=kext:
                                    e.tensor_scalar(
                        out=pbuf[:, 2048:2048 + kext],
                        in0=pbuf[:, 2048:2048 + kext],
                        scalar1=cols[:, RINV + 1:RINV + 2], scalar2=None,
                        op0=ALU.mult), "tv", 1)

            ev_ts = {}
            ptb_ev = {}
            pb_i = [0]

            def emit_transposes(it, qg=qg, iters=iters, pn_tv=pn_tv):
                h, qi = iters[it]
                ip = it % 2
                qt = 4 * qg + qi
                nkb = qt + 1
                wait("tensor", "tv", pn_tv[it])
                esc = False
                for b0 in range(0, nkb, 8):
                    bn = min(8, nkb - b0)
                    bi = pb_i[0]
                    pb_i[0] += 1
                    slot = 1024 * (bi % 2)
                    g = ptb_ev.get(bi - 2)
                    if g:
                        wait("tensor", g[0], g[1])
                    for b in range(bn):
                        inc("tensor", lambda e, ip=ip, b0=b0, b=b, slot=slot:
                            e.transpose(
                            pbT[:, slot + 128 * b:slot + 128 * b + 128],
                            pbuf[:, 2048 * ip + 128 * (b0 + b):
                                 2048 * ip + 128 * (b0 + b) + 128],
                            identb[:]), "tt", 1)
                    ttv = C["tt"]
                    if esc:
                        wait("scalar", "tt", ttv)
                        v = inc("scalar", lambda e, b0=b0, bn=bn, slot=slot:
                                e.activation(
                            out=ptb[:, 128 * b0:128 * b0 + 128 * bn],
                            in_=pbT[:, slot:slot + 128 * bn],
                            func=ACT.Copy), "ts", 1)
                        ptb_ev[bi] = ("ts", v)
                    else:
                        wait("vector", "tt", ttv)
                        v = inc("vector", lambda e, b0=b0, bn=bn, slot=slot:
                                e.tensor_copy(
                            out=ptb[:, 128 * b0:128 * b0 + 128 * bn],
                            in_=pbT[:, slot:slot + 128 * bn]), "tv", 1)
                        ptb_ev[bi] = ("tv", v)
                tpP[ip] = C["tt"]
                ev_ts[it] = ("ts", C["ts"]) if esc else ("tv", C["tv"])

            def emit_pv(it, qg=qg, iters=iters):
                h, qi = iters[it]
                qt = 4 * qg + qi
                nkb = qt + 1
                wait("tensor", ev_ts[it][0], ev_ts[it][1])
                for kb in range(nkb):
                    inc("tensor", lambda e, kb=kb, h=h, qi=qi, nkb=nkb: e.matmul(
                        pbA[0:64, 128 * qi:128 * qi + 128],
                        vt[:, 1024 * kb + 64 * h:1024 * kb + 64 * h + 64],
                        ptb[:, 128 * kb:128 * kb + 128],
                        start=(kb == 0),
                        stop=(kb == nkb - 1)), "tt", 1)
                if qi == 3:
                    ttv = C["tt"]
                    wait("vector", "tt", ttv)
                    tsa = inc("vector", lambda e, h=h: e.tensor_copy(
                        out=asb[64 * (h % 2):64 * (h % 2) + 64,
                                512 * (h // 2):512 * (h // 2) + 512],
                        in_=pbA[0:64, 0:512]), "tv", 1)
                    wait("tensor", "tv", tsa)

            emit_scores_exp(0)
            emit_vec(0)
            for it in range(64):
                if it > 0:
                    emit_transposes(it - 1)
                    emit_pv(it - 1)
                if it + 1 < 64:
                    emit_scores_exp(it + 1)
                    emit_vec(it + 1)
            emit_transposes(63)
            emit_pv(63)
            for mh in range(2):
                wait("gpsimd", "tt", C["tt"])
                lgv = 0
                for c in range(NC):
                    lgv = inc("gpsimd", lambda e, c=c, mh=mh: e.dma_start(
                        out=wgrp[:, 512 * c:512 * c + 512],
                        in_=wo_d.ap()[128 * c:128 * c + 128,
                                      512 * mh:512 * mh + 512]), "lg", 16)
                wait("tensor", "lg", lgv)
                for mm in range(4):
                    m = 4 * mh + mm
                    for c in range(NC):
                        inc("tensor", lambda e, m=m, mm=mm, c=c: e.matmul(
                            pbY[:, 0:512],
                            wgrp[:, 512 * c + 128 * mm:512 * c + 128 * mm + 128],
                            asb[:, 512 * c:512 * c + 512],
                            start=(c == 0), stop=(c == NC - 1)), "tt", 1)
                    ttv = C["tt"]
                    wait("vector", "tt", ttv)
                    tvy = inc("vector", lambda e, m=m: e.tensor_scalar(
                        out=ysb[:, 512 * m:512 * m + 512], in0=pbY[:, 0:512],
                        scalar1=wob[:, m:m + 1], scalar2=None, op0=ALU.add), "tv", 1)
                    wait("tensor", "tv", tvy)
            tvv = inc("vector", lambda e: e.nop(), "tv", 1)
            ttv = stats_mm(ysb, tvv)
            wait("vector", "tt", ttv)
            tvv = inc("vector", lambda e: e.tensor_scalar(
                out=sr(0), in0=pbB[:, 0:512], scalar1=1.0 / D, scalar2=None,
                op0=ALU.mult), "tv", 1)
            wait("tensor", "tv", tvv)
            ttv = stats_sq_mm(ysb, False)
            wait("vector", "tt", ttv)
            raw("vector", lambda e: e.tensor_scalar(
                out=sr(2), in0=pbB[:, 0:512], scalar1=1.0 / D, scalar2=None,
                op0=ALU.mult))
            drain("vector")
            raw("vector", lambda e: e.tensor_tensor(
                out=sr(5), in0=sr(0), in1=sr(0), op=ALU.mult))
            drain("vector")
            tvv = inc("vector", lambda e: e.tensor_tensor(
                out=sr(2), in0=sr(2), in1=sr(5), op=ALU.subtract), "tv", 1)
            wait("scalar", "tv", tvv)
            raw("scalar", lambda e: e.activation(out=sr(1), in_=sr(2), func=ACT.Ln, bias=epscol[:, 0:1]))
            tsv = inc("scalar", lambda e: e.activation(
                out=sr(1), in_=sr(1), func=ACT.Exp, scale=-0.5), "ts", 1)
            wait("vector", "ts", tsv)
            tvv = inc("vector", lambda e: e.scalar_tensor_tensor(
                out=sr(0), in0=sr(0), scalar=-1.0, in1=sr(1),
                op0=ALU.mult, op1=ALU.mult), "tv", 1)
            wait("vector", "lx", lx_x)
            for c in range(NC):
                raw("vector", lambda e, c=c: e.tensor_tensor(
                    out=scrA[:, 0:512], in0=ysb[:, 512 * c:512 * c + 512],
                    in1=sr(1), op=ALU.mult))
                drain("vector")
                raw("vector", lambda e: e.tensor_tensor(
                    out=scrA[:, 0:512], in0=scrA[:, 0:512], in1=sr(0), op=ALU.add))
                drain("vector")
                raw("vector", lambda e, c=c: e.tensor_tensor(
                    out=xg[:, 512 * c:512 * c + 512],
                    in0=xg[:, 512 * c:512 * c + 512],
                    in1=scrA[:, 0:512], op=ALU.add))
                drain("vector")
            tv_x2 = inc("vector", lambda e: e.nop(), "tv", 1)
            wait("sync", "tv", tv_x2)
            inc("sync", lambda e, qg=qg: e.dma_start(
                out=x2_d.ap().rearrange("(c p) q -> p c q", p=128)[
                    :, :, 512 * qg:512 * qg + 512],
                in_=xg[:].rearrange("p (c q) -> p c q", c=NC)), "lx", 16)
            ttv = stats_copy_mm(xg)
            wait("vector", "tt", ttv)
            tvv = inc("vector", lambda e: e.tensor_scalar(
                out=sr(0), in0=pbB[:, 0:512], scalar1=1.0 / D, scalar2=None,
                op0=ALU.mult), "tv", 1)
            wait("tensor", "tv", tvv)
            ttv = stats_sq_mm(xg, True)
            wait("vector", "tt", ttv)
            raw("vector", lambda e: e.tensor_scalar(
                out=sr(2), in0=pbB[:, 0:512], scalar1=1.0 / D, scalar2=None,
                op0=ALU.mult))
            drain("vector")
            raw("vector", lambda e: e.tensor_tensor(
                out=sr(5), in0=sr(0), in1=sr(0), op=ALU.mult))
            drain("vector")
            tvv = inc("vector", lambda e: e.tensor_tensor(
                out=sr(2), in0=sr(2), in1=sr(5), op=ALU.subtract), "tv", 1)
            wait("scalar", "tv", tvv)
            raw("scalar", lambda e: e.activation(out=sr(1), in_=sr(2), func=ACT.Ln, bias=epscol[:, 0:1]))
            tsv = inc("scalar", lambda e: e.activation(
                out=sr(1), in_=sr(1), func=ACT.Exp, scale=-0.5), "ts", 1)
            wait("vector", "ts", tsv)
            raw("vector", lambda e: e.tensor_tensor(
                out=sr(5), in0=sr(1), in1=sr(1), op=ALU.mult))
            drain("vector")
            tvv = inc("vector", lambda e: e.tensor_tensor(
                out=sr(5), in0=sr(5), in1=sr(2), op=ALU.mult), "tv", 1)
            wait("scalar", "tv", tvv)
            raw("scalar", lambda e: e.activation(out=sr(3), in_=sr(5), func=ACT.Ln, bias=epscol[:, 0:1]))
            tsv = inc("scalar", lambda e: e.activation(
                out=sr(3), in_=sr(3), func=ACT.Exp, scale=-0.5), "ts", 1)
            wait("vector", "ts", tsv)
            raw("vector", lambda e: e.tensor_tensor(
                out=sr(3), in0=sr(3), in1=sr(1), op=ALU.mult))
            drain("vector")
            tvv = inc("vector", lambda e: e.scalar_tensor_tensor(
                out=sr(4), in0=sr(0), scalar=-1.0, in1=sr(3),
                op0=ALU.mult, op1=ALU.mult), "tv", 1)
            for c in range(NC):
                raw("vector", lambda e, c=c: e.tensor_scalar(
                    out=sr(5), in0=sr(3), scalar1=lng[:, c:c + 1],
                    scalar2=None, op0=ALU.mult))
                drain("vector")
                raw("vector", lambda e, c=c: e.tensor_tensor(
                    out=scrA[:, 0:512], in0=xg[:, 512 * c:512 * c + 512],
                    in1=sr(5), op=ALU.mult))
                drain("vector")
                raw("vector", lambda e, c=c: e.tensor_scalar(
                    out=sr(5), in0=sr(4), scalar1=lng[:, c:c + 1],
                    scalar2=lnb[:, c:c + 1], op0=ALU.mult, op1=ALU.add))
                drain("vector")
                raw("vector", lambda e, c=c: e.tensor_tensor(
                    out=h1sb[:, 512 * c:512 * c + 512], in0=scrA[:, 0:512],
                    in1=sr(5), op=ALU.add))
                drain("vector")
            raw("vector", lambda e: e.reduce_max(
                out=cols[:, CTMP:CTMP + 1], in_=h1sb[:, 0:4096:4], axis=AX,
                apply_absolute_value=True))
            raw("vector", lambda e: e.tensor_tensor(
                out=cols[:, 16:17], in0=cols[:, 16:17],
                in1=cols[:, CTMP:CTMP + 1], op=ALU.max))
            if q1ss_lx[0]:
                wait("vector", "lx", q1ss_lx[0])
            for c in range(NC):
                raw("vector", lambda e, c=c: e.tensor_copy(
                    out=scrA[:, 2048 + 64 * c:2048 + 64 * c + 64],
                    in_=h1sb[:, 512 * c:512 * c + 512:8]))
            tv_h1 = inc("vector", lambda e: e.nop(), "tv", 1)
            wait("sync", "tv", tv_h1)
            inc("sync", lambda e, qg=qg: e.dma_start(
                out=h1_d.ap().rearrange("(c p) q -> p c q", p=128)[
                    :, :, 512 * qg:512 * qg + 512],
                in_=h1sb[:].rearrange("p (c q) -> p c q", c=NC)), "lx", 16)
            q1ss_lx[0] = inc("sync", lambda e, qg=qg: e.dma_start(
                out=h1ss_d.ap()[:, 512 * qg:512 * qg + 512],
                in_=scrA[:, 2048:2560]), "lx", 16)
            if qg == 0:
                emit_tern_some(24)
            elif qg == 1:
                emit_tern_some(8)
                emit_gamma_finalize()
                emit_tern_some(8)
            else:
                emit_tern_some(24)
        emit_tern_some(len(tern_jobs))

        def part_reduce_max_col(col_ap, bounce_d):
            tvv = inc("vector", lambda e: e.nop(), "tv", 1)
            wait("gpsimd", "tv", tvv)
            lgv = inc("gpsimd", lambda e, col_ap=col_ap, bounce_d=bounce_d:
                      e.dma_start(out=bounce_d.ap(), in_=col_ap), "lg", 16)
            wait("gpsimd", "lg", lgv)
            lgv = inc("gpsimd", lambda e, bounce_d=bounce_d: e.dma_start(
                out=wstr[0:1, 1024:1152],
                in_=bounce_d.ap().rearrange("(a p) x -> a (p x)", a=1)), "lg", 16)
            wait("vector", "lg", lgv)
            raw("vector", lambda e: e.reduce_max(out=rows[0:1, 0:1], in_=wstr[0:1, 1024:1152], axis=AX))
            drain("vector")

        def allgather_row(src_off, nvals, tag):
            tvv = inc("vector", lambda e: e.nop(), "tv", 1)
            wait("gpsimd", "tv", tvv)
            lgv = inc("gpsimd", lambda e, src_off=src_off, nvals=nvals, tag=tag:
                      e.dma_start(out=ccd[tag][0].ap(),
                                  in_=rows[0:1, src_off:src_off + nvals]), "lg", 16)
            wait("gpsimd", "lg", lgv)
            ccv = inc("gpsimd", lambda e, tag=tag: e.collective_compute(
                "AllGather", ALU.bypass, replica_groups=[list(range(NCORE))],
                ins=[ccd[tag][0].ap()], outs=[ccd[tag][1].ap()]), "cc", 1)
            wait("gpsimd", "cc", ccv)
            lgv = inc("gpsimd", lambda e, tag=tag, nvals=nvals: e.dma_start(
                out=wstr[0:1, 0:NCORE * nvals], in_=ccd[tag][1].ap()), "lg", 16)
            wait("vector", "lg", lgv)

        def bcast_row_to_tile(src_off, nvals, dst_tile):
            tvv = inc("vector", lambda e: e.nop(), "tv", 1)
            wait("tensor", "tv", tvv)
            ttv = inc("tensor", lambda e, src_off=src_off, nvals=nvals: e.matmul(
                pbB[:, 0:nvals], onesrow[:], rows[0:1, src_off:src_off + nvals],
                start=True, stop=True), "tt", 1)
            wait("vector", "tt", ttv)
            return inc("vector", lambda e, dst_tile=dst_tile, nvals=nvals:
                       e.tensor_copy(out=dst_tile[:, 0:nvals], in_=pbB[:, 0:nvals]),
                       "tv", 1)

        def count_resident(tile, dummy, thr_tile, cnt_tile, nthr):
            for j in range(nthr):
                raw("vector", lambda e, j=j, tile=tile, dummy=dummy,
                    thr_tile=thr_tile, cnt_tile=cnt_tile: e.tensor_scalar(
                    out=dummy, in0=tile,
                    scalar1=thr_tile[:, j:j + 1], scalar2=None,
                    op0=ALU.is_gt, op1=ALU.add,
                    accum_out=cnt_tile[:, j:j + 1]))
            return inc("vector", lambda e: e.nop(), "tv", 1)

        def count_finish(cnt_tile, nthr, tag, r_target):
            tvv = inc("vector", lambda e: e.nop(), "tv", 1)
            wait("tensor", "tv", tvv)
            ttv = inc("tensor", lambda e, cnt_tile=cnt_tile, nthr=nthr: e.matmul(
                pbB[:, 0:nthr], onesf[:], cnt_tile[:, 0:nthr],
                start=True, stop=True), "tt", 1)
            wait("vector", "tt", ttv)
            tvv = inc("vector", lambda e, nthr=nthr: e.tensor_copy(
                out=rows[0:1, 32:32 + nthr], in_=pbB[0:1, 0:nthr]), "tv", 1)
            allgather_row(32, nthr, tag)
            raw("vector", lambda e, nthr=nthr: e.tensor_copy(
                out=rows[0:1, 32:32 + nthr], in_=wstr[0:1, 0:nthr]))
            drain("vector")
            for cc_i in range(1, NCORE):
                raw("vector", lambda e, cc_i=cc_i, nthr=nthr: e.tensor_tensor(
                    out=rows[0:1, 32:32 + nthr], in0=rows[0:1, 32:32 + nthr],
                    in1=wstr[0:1, nthr * cc_i:nthr * cc_i + nthr], op=ALU.add))
                drain("vector")
            raw("vector", lambda e, nthr=nthr, r_target=r_target: e.tensor_scalar(
                out=rows[0:1, 64:64 + nthr], in0=rows[0:1, 32:32 + nthr],
                scalar1=float(r_target), scalar2=None, op0=ALU.is_gt))
            drain("vector")
            raw("vector", lambda e: e.memset(rows[0:1, 64:65], 1.0))
            drain("vector")
            raw("vector", lambda e, nthr=nthr: e.memset(rows[0:1, 64 + nthr - 1:64 + nthr], 0.0))
            drain("vector")
            raw("vector", lambda e, nthr=nthr: e.tensor_tensor(
                out=rows[0:1, 80:80 + nthr - 1], in0=rows[0:1, 64:64 + nthr - 1],
                in1=rows[0:1, 65:64 + nthr], op=ALU.subtract))
            drain("vector")

            def msum(dst, src, shift, nthr=nthr):
                raw("vector", lambda e, dst=dst, src=src, shift=shift: e.tensor_tensor(
                    out=rows[0:1, 96:96 + nthr - 1],
                    in0=rows[0:1, src + shift:src + shift + nthr - 1],
                    in1=rows[0:1, 80:80 + nthr - 1], op=ALU.mult))
                drain("vector")
                raw("vector", lambda e, dst=dst: e.reduce_sum(
                    out=rows[0:1, dst:dst + 1], in_=rows[0:1, 96:96 + nthr - 1],
                    axis=AX))
                drain("vector")
            msum(2, 48, 0)
            msum(3, 48, 1)
            msum(4, 32, 0)
            msum(5, 32, 1)
            raw("vector", lambda e: e.tensor_tensor(
                out=rows[0:1, 6:7], in0=rows[0:1, 4:5], in1=rows[0:1, 5:6],
                op=ALU.subtract))
            drain("vector")
            raw("vector", lambda e: e.tensor_scalar(
                out=rows[0:1, 6:7], in0=rows[0:1, 6:7], scalar1=1e-9,
                scalar2=None, op0=ALU.max))
            drain("vector")
            raw("vector", lambda e: e.reciprocal(out=rows[0:1, 7:8], in_=rows[0:1, 6:7]))
            drain("vector")
            raw("vector", lambda e, r_target=r_target: e.tensor_scalar(
                out=rows[0:1, 8:9], in0=rows[0:1, 4:5], scalar1=float(-r_target),
                scalar2=None, op0=ALU.add))
            drain("vector")
            raw("vector", lambda e: e.tensor_tensor(
                out=rows[0:1, 8:9], in0=rows[0:1, 8:9], in1=rows[0:1, 7:8],
                op=ALU.mult))
            drain("vector")
            raw("vector", lambda e: e.tensor_tensor(
                out=rows[0:1, 9:10], in0=rows[0:1, 3:4], in1=rows[0:1, 2:3],
                op=ALU.subtract))
            drain("vector")
            raw("vector", lambda e: e.tensor_tensor(
                out=rows[0:1, 9:10], in0=rows[0:1, 9:10], in1=rows[0:1, 8:9],
                op=ALU.mult))
            drain("vector")
            raw("vector", lambda e: e.tensor_tensor(
                out=rows[0:1, 0:1], in0=rows[0:1, 2:3], in1=rows[0:1, 9:10],
                op=ALU.add))
            drain("vector")
            return inc("vector", lambda e: e.nop(), "tv", 1)

        def emit_quantile(which):
            acol = 16 if which == 1 else 17
            bounce = amx1_d if which == 1 else amx2_d
            tags = ("a1", "b1", "c1") if which == 1 else ("a2", "b2", "c2")
            r_t = R1 if which == 1 else R2
            cnt_b, cnt_f = bcnt, (cnt1 if which == 1 else cnt2)
            tile = scrA[:, 0:2048] if which == 1 else scrA[:, 2048:4096]
            dummy = h1sb[:, 0:2048]
            wait("vector", "lx", C["lx"])
            if which == 1:
                wait("sync", "tv", C["tv"])
                lxv = inc("sync", lambda e: e.dma_start(
                    out=scrA[:, 0:2048], in_=h1ss_d.ap()), "lx", 16)
                wait("scalar", "lx", lxv)
            wait("scalar", "tv", C["tv"])
            tsv = inc("scalar", lambda e, tile=tile: e.activation(
                out=tile, in_=tile, func=ACT.Abs), "ts", 1)
            wait("vector", "ts", tsv)

            def count_pass(thr_tile, cnt_tile, nthr, tile=tile, dummy=dummy):
                return count_resident(tile, dummy, thr_tile, cnt_tile, nthr)
            part_reduce_max_col(cols[:, acol:acol + 1], bounce)
            raw("vector", lambda e: e.memset(rows[0:1, 1:2], 0.0))
            drain("vector")
            allgather_row(0, 2, tags[0])
            raw("vector", lambda e: e.reduce_max(
                out=rows[0:1, 0:1], in_=wstr[0:1, 0:2 * NCORE], axis=AX))
            drain("vector")
            raw("vector", lambda e: e.tensor_scalar(
                out=rows[0:1, 48:48 + NBOOT], in0=bootf[:], scalar1=rows[0:1, 0:1],
                scalar2=None, op0=ALU.mult))
            drain("vector")
            raw("vector", lambda e: e.tensor_copy(
                out=grow[0:1, 0:NBOOT], in_=rows[0:1, 48:48 + NBOOT]))
            drain("vector")
            raw("vector", lambda e, cnt_b=cnt_b: e.memset(cnt_b[:], 0.0))
            drain("vector")
            bcast_row_to_tile(48, NBOOT, thrb)
            count_pass(thrb, cnt_b, NBOOT)
            raw("vector", lambda e: e.tensor_copy(
                out=rows[0:1, 48:48 + NBOOT], in_=grow[0:1, 0:NBOOT]))
            drain("vector")
            count_finish(cnt_b, NBOOT, tags[1], r_t)
            raw("vector", lambda e: e.tensor_scalar(
                out=rows[0:1, 48:48 + NFINE], in0=finef[:], scalar1=rows[0:1, 0:1],
                scalar2=None, op0=ALU.mult))
            drain("vector")
            raw("vector", lambda e: e.tensor_copy(
                out=grow[0:1, 8:8 + NFINE], in_=rows[0:1, 48:48 + NFINE]))
            drain("vector")
            raw("vector", lambda e, cnt_f=cnt_f: e.memset(cnt_f[:], 0.0))
            drain("vector")
            bcast_row_to_tile(48, NFINE, thrf)
            count_pass(thrf, cnt_f, NFINE)
            raw("vector", lambda e: e.tensor_copy(
                out=rows[0:1, 48:48 + NFINE], in_=grow[0:1, 8:8 + NFINE]))
            drain("vector")
            count_finish(cnt_f, NFINE, tags[2], r_t)
            raw("vector", lambda e: e.tensor_scalar(
                out=rows[0:1, 1:2], in0=rows[0:1, 0:1], scalar1=1e-5,
                scalar2=None, op0=ALU.max))
            drain("vector")
            raw("vector", lambda e: e.reciprocal(out=rows[0:1, 2:3], in_=rows[0:1, 1:2]))
            drain("vector")
            raw("vector", lambda e: e.tensor_scalar(
                out=rows[0:1, 2:3], in0=rows[0:1, 2:3], scalar1=127.0,
                scalar2=None, op0=ALU.mult))
            drain("vector")
            return _bcast_sv()

        def _bcast_sv():
            tvv = inc("vector", lambda e: e.nop(), "tv", 1)
            wait("tensor", "tv", tvv)
            ttv = inc("tensor", lambda e: e.matmul(
                pbB[:, 0:3], onesrow[:], rows[0:1, 0:3], start=True, stop=True),
                "tt", 1)
            wait("vector", "tt", ttv)
            return inc("vector", lambda e: e.tensor_copy(
                out=svals[:, 4:7], in_=pbB[:, 0:3]), "tv", 1)

        emit_quantile(1)
        drain("vector")
        raw("vector", lambda e: e.tensor_copy(
            out=svals[:, SV_S1:SV_S1 + 1], in_=svals[:, 6:7]))
        drain("vector")
        raw("vector", lambda e: e.tensor_tensor(
            out=svals[:, SV_GS:SV_GS + 1], in0=svals[:, 5:6], in1=cols[:, 14:15],
            op=ALU.mult))
        drain("vector")
        tvv = inc("vector", lambda e: e.tensor_scalar(
            out=svals[:, SV_GS:SV_GS + 1], in0=svals[:, SV_GS:SV_GS + 1],
            scalar1=1.0 / 127.0, scalar2=None, op0=ALU.mult), "tv", 1)
        q_lx = {}
        for c in range(NC + 1):
            if c < NC:
                if c >= 2 or c == 0:
                    wait("sync", "tv", C["tv"])
                q_lx[c] = inc("sync", lambda e, c=c: e.dma_start(
                    out=(scrA[:, 0:2048] if c % 2 == 0 else h1sb[:, 0:2048]),
                    in_=h1_d.ap()[128 * c:128 * c + 128, :]), "lx", 16)
            cj = c - 1
            if 0 <= cj < NC:
                buf = scrA[:, 0:2048] if cj % 2 == 0 else h1sb[:, 0:2048]
                wait("vector", "lx", q_lx[cj])
                raw("vector", lambda e, buf=buf: e.tensor_scalar(
                    out=wstr[:, 0:2048], in0=buf,
                    scalar1=svals[:, SV_S1:SV_S1 + 1], scalar2=MAGIC,
                    op0=ALU.mult, op1=ALU.add))
                raw("vector", lambda e: e.tensor_scalar(
                    out=wstr[:, 0:2048], in0=wstr[:, 0:2048], scalar1=MAGIC,
                    scalar2=127.0, op0=ALU.subtract, op1=ALU.min))
                raw("vector", lambda e, cj=cj: e.tensor_scalar(
                    out=n1[:, 2048 * cj:2048 * cj + 2048], in0=wstr[:, 0:2048],
                    scalar1=-128.0, scalar2=None, op0=ALU.max))
        tv_n1 = inc("vector", lambda e: e.nop(), "tv", 1)

        t1a, t1b = qf, kf
        wait("gpsimd", "tt", C["tt"])
        wait("gpsimd", "lx", C["lx"])
        for ch in range(NC):
            dst = t1a if ch < 4 else t1b
            off = 4096 * (ch % 4)
            inc("gpsimd", lambda e, ch=ch, dst=dst, off=off: e.dma_start(
                out=dst[:, off:off + 4096],
                in_=t1_d.ap()[128 * ch:128 * ch + 128, :]), "lg", 16)
        lg_t1 = C["lg"]
        wait("tensor", "lg", lg_t1)
        wait("tensor", "tv", tv_n1)
        wait("tensor", "ts", C["ts"])
        wait("scalar", "tv", tv_n1)
        h2dma = [0, 0, 0, 0]
        gelu_ts = [0, 0, 0, 0]
        sub_tv = [0, 0, 0, 0]
        for qg4 in range(NQG):
            for m in range(NM1):
                ms = m % 4
                if gelu_ts[ms]:
                    wait("tensor", "ts", gelu_ts[ms])
                for k in range(NC):
                    src = t1a if k < 4 else t1b
                    off = 4096 * (k % 4)
                    inc("tensor", lambda e, m=m, ms=ms, k=k, qg4=qg4, src=src,
                        off=off: e.matmul(
                        pbS[:, 512 * ms:512 * ms + 512],
                        src[:, off + 128 * m:off + 128 * m + 128],
                        n1[:, 2048 * k + 512 * qg4:2048 * k + 512 * qg4 + 512],
                        start=(k == 0), stop=(k == NC - 1)), "tt", 1)
                ttv = C["tt"]
                wait("scalar", "tt", ttv)
                if h2dma[ms]:
                    wait("scalar", "lx", h2dma[ms])
                if sub_tv[ms]:
                    wait("scalar", "tv", sub_tv[ms])
                gelu_ts[ms] = inc("scalar", lambda e, ms=ms: e.activation(
                    out=scrA[:, 512 * ms:512 * ms + 512],
                    in_=pbS[:, 512 * ms:512 * ms + 512],
                    func=ACT.Gelu, scale=svals[:, SV_GS:SV_GS + 1]), "ts", 1)
                wait("sync", "ts", gelu_ts[ms])
                h2dma[ms] = inc("sync", lambda e, m=m, qg4=qg4, ms=ms: e.dma_start(
                    out=h2_d.ap()[128 * m:128 * m + 128, 512 * qg4:512 * qg4 + 512],
                    in_=scrA[:, 512 * ms:512 * ms + 512]), "lx", 16)
                wait("vector", "ts", gelu_ts[ms])
                raw("vector", lambda e, ms=ms, m=m, qg4=qg4: e.tensor_copy(
                    out=scrA[:, 2048 + 64 * m + 16 * qg4:
                             2048 + 64 * m + 16 * qg4 + 16],
                    in_=scrA[:, 512 * ms:512 * ms + 512:32]))
                if m % 4 == 0:
                    raw("vector", lambda e, ms=ms: e.reduce_max(
                        out=cols[:, CTMP:CTMP + 1],
                        in_=scrA[:, 512 * ms:512 * ms + 512:4], axis=AX,
                        apply_absolute_value=True))
                    raw("vector", lambda e: e.tensor_tensor(
                        out=cols[:, 17:18], in0=cols[:, 17:18],
                        in1=cols[:, CTMP:CTMP + 1], op=ALU.max))
                sub_tv[ms] = C["tv"]

        wait("vector", "lx", C["lx"])
        emit_quantile(2)
        drain("vector")
        raw("vector", lambda e: e.tensor_copy(
            out=svals[:, SV_S2:SV_S2 + 1], in_=svals[:, 6:7]))
        drain("vector")
        raw("vector", lambda e: e.tensor_tensor(
            out=cols[:, 18:19], in0=svals[:, 5:6], in1=cols[:, 15:16], op=ALU.mult))
        drain("vector")
        raw("vector", lambda e: e.tensor_scalar(
            out=cols[:, 18:19], in0=cols[:, 18:19], scalar1=1.0 / 127.0,
            scalar2=None, op0=ALU.mult))
        drain("vector")
        raw("vector", lambda e: e.tensor_tensor(
            out=cols[:, 19:20], in0=cols[:, 18:19], in1=cols[:, 18:19], op=ALU.mult))
        drain("vector")
        raw("vector", lambda e: e.reciprocal(out=cols[:, 19:20], in_=cols[:, 19:20]))
        drain("vector")
        tvv = inc("vector", lambda e: e.tensor_scalar(
            out=svals[:, SV_EPS5:SV_EPS5 + 1], in0=cols[:, 19:20], scalar1=EPS,
            scalar2=None, op0=ALU.mult), "tv", 1)
        tv_s2 = C["tv"]

        wait("gpsimd", "tt", C["tt"])
        wait("gpsimd", "lx", C["lx"])
        for ch in range(32):
            dst = t1a if ch < 16 else t1b
            off = 1024 * (ch % 16)
            inc("gpsimd", lambda e, ch=ch, dst=dst, off=off: e.dma_start(
                out=dst[:, off:off + 1024],
                in_=t2_d.ap()[128 * ch:128 * ch + 128, :]), "lg", 16)
        lg_t2 = C["lg"]
        wait("tensor", "lg", lg_t2)
        wait("tensor", "tv", tv_s2)
        n2 = vt
        for qg5 in range(NQG):
            q_tv = {}
            for hi in range(8):
                quarter, hh = hi // 2, hi % 2
                sl = 2048 * (hi % 2)
                if hi >= 2:
                    wait("sync", "tv", q_tv[hi - 2])
                else:
                    wait("sync", "tv", C["tv"])
                lxv = inc("sync", lambda e, qg5=qg5, quarter=quarter, hh=hh,
                          sl=sl: e.dma_start(
                    out=scrA[:, sl:sl + 2048].rearrange("p (a q) -> p a q", a=4),
                    in_=h2_d.ap().rearrange("(a p) q -> p a q", p=128)[
                        :, 8 * quarter + 4 * hh:8 * quarter + 4 * hh + 4,
                        512 * qg5:512 * qg5 + 512]), "lx", 16)
                wait("vector", "lx", lxv)
                raw("vector", lambda e, sl=sl: e.tensor_scalar(
                    out=wstr[:, 0:2048], in0=scrA[:, sl:sl + 2048],
                    scalar1=svals[:, SV_S2:SV_S2 + 1], scalar2=MAGIC,
                    op0=ALU.mult, op1=ALU.add))
                raw("vector", lambda e: e.tensor_scalar(
                    out=wstr[:, 0:2048], in0=wstr[:, 0:2048], scalar1=MAGIC,
                    scalar2=127.0, op0=ALU.subtract, op1=ALU.min))
                q_tv[hi] = inc("vector", lambda e, hi=hi: e.tensor_scalar(
                    out=n2[:, 2048 * hi:2048 * hi + 2048],
                    in0=wstr[:, 0:2048], scalar1=-128.0, scalar2=None,
                    op0=ALU.max), "tv", 1)
            u_ts = [0, 0, 0, 0]
            u_tv = [0, 0, 0, 0]
            for mg in range(2):
                for hi in range(8):
                    wait("tensor", "tv", q_tv[hi])
                    for mi in range(4):
                        m = 4 * mg + mi
                        ms = mi
                        if hi == 0:
                            if u_ts[ms]:
                                wait("tensor", "ts", u_ts[ms])
                            if u_tv[ms]:
                                wait("tensor", "tv", u_tv[ms])
                        for k in range(4 * hi, 4 * hi + 4):
                            src = t1a if k < 16 else t1b
                            off = 1024 * (k % 16)
                            inc("tensor", lambda e, m=m, ms=ms, k=k, src=src,
                                off=off: e.matmul(
                                pbS[:, 512 * ms:512 * ms + 512],
                                src[:, off + 128 * m:off + 128 * m + 128],
                                n2[:, 512 * k:512 * k + 512],
                                start=(k == 0), stop=(k == 31)), "tt", 1)
                        if hi == 7:
                            ttv = C["tt"]
                            wait("scalar", "tt", ttv)
                            u_ts[ms] = inc("scalar", lambda e, m=m, ms=ms:
                                           e.activation(
                                out=h1sb[:, 512 * m:512 * m + 512],
                                in_=pbS[:, 512 * ms:512 * ms + 512],
                                func=ACT.Copy), "ts", 1)
                            wait("vector", "ts", u_ts[ms])
                            u_tv[ms] = inc("vector", lambda e, m=m:
                                           e.tensor_copy(
                                out=ysb[:, 512 * m:512 * m + 512],
                                in_=h1sb[:, 512 * m:512 * m + 512]), "tv", 1)
            wait("vector", "ts", C["ts"])
            tvv = inc("vector", lambda e: e.nop(), "tv", 1)
            ttv = stats_mm(ysb, tvv)
            wait("vector", "tt", ttv)
            tvv = inc("vector", lambda e: e.tensor_scalar(
                out=sr(0), in0=pbB[:, 0:512], scalar1=1.0 / D, scalar2=None,
                op0=ALU.mult), "tv", 1)
            wait("tensor", "tv", tvv)
            ttv = stats_sq_mm(ysb, False)
            wait("vector", "tt", ttv)
            raw("vector", lambda e: e.tensor_scalar(
                out=sr(2), in0=pbB[:, 0:512], scalar1=1.0 / D, scalar2=None,
                op0=ALU.mult))
            drain("vector")
            raw("vector", lambda e: e.tensor_tensor(
                out=sr(5), in0=sr(0), in1=sr(0), op=ALU.mult))
            drain("vector")
            raw("vector", lambda e: e.tensor_tensor(
                out=sr(2), in0=sr(2), in1=sr(5), op=ALU.subtract))
            drain("vector")
            tvv = inc("vector", lambda e: e.tensor_scalar(
                out=sr(2), in0=sr(2), scalar1=svals[:, SV_EPS5:SV_EPS5 + 1],
                scalar2=None, op0=ALU.add), "tv", 1)
            wait("scalar", "tv", tvv)
            raw("scalar", lambda e: e.activation(out=sr(1), in_=sr(2), func=ACT.Ln, bias=0.0))
            tsv = inc("scalar", lambda e: e.activation(
                out=sr(1), in_=sr(1), func=ACT.Exp, scale=-0.5), "ts", 1)
            wait("vector", "ts", tsv)
            tvv = inc("vector", lambda e: e.scalar_tensor_tensor(
                out=sr(4), in0=sr(0), scalar=-1.0, in1=sr(1),
                op0=ALU.mult, op1=ALU.mult), "tv", 1)
            wait("sync", "tv", tvv)
            wait("sync", "so", C["so"])
            lx_x2 = inc("sync", lambda e, qg5=qg5: e.dma_start(
                out=xg[:].rearrange("p (c q) -> p c q", c=NC),
                in_=x2_d.ap().rearrange("(c p) q -> p c q", p=128)[
                    :, :, 512 * qg5:512 * qg5 + 512]), "lx", 16)
            wait("vector", "lx", lx_x2)
            for c in range(NC):
                raw("vector", lambda e, c=c: e.tensor_tensor(
                    out=wstr[:, 0:512], in0=h1sb[:, 512 * c:512 * c + 512],
                    in1=sr(1), op=ALU.mult))
                drain("vector")
                raw("vector", lambda e: e.tensor_tensor(
                    out=wstr[:, 0:512], in0=wstr[:, 0:512], in1=sr(4), op=ALU.add))
                drain("vector")
                raw("vector", lambda e, c=c: e.tensor_tensor(
                    out=xg[:, 512 * c:512 * c + 512],
                    in0=xg[:, 512 * c:512 * c + 512],
                    in1=wstr[:, 0:512], op=ALU.add))
                drain("vector")
            tv_out = inc("vector", lambda e: e.nop(), "tv", 1)
            wait("sync", "tv", tv_out)
            inc("sync", lambda e, qg5=qg5: e.dma_start(
                out=out_d.ap().rearrange("(c p) q -> p c q", p=128)[
                    :, :, 512 * qg5:512 * qg5 + 512],
                in_=xg[:].rearrange("p (c q) -> p c q", c=NC)), "so", 16)
        wait("sync", "so", C["so"])

        block = E(nc.Block())

        @block.sync
        def _(e):
            for f in EM["sync"]:
                f(e)

        @block.gpsimd
        def _(e):
            for f in EM["gpsimd"]:
                f(e)

        @block.tensor
        def _(e):
            for f in EM["tensor"]:
                f(e)

        @block.vector
        def _(e):
            for f in EM["vector"]:
                f(e)

        @block.scalar
        def _(e):
            for f in EM["scalar"]:
                f(e)

    return nc


_NC_CACHE = {}


def kernel(L0_attn_inputs, L0_attn_Wqkv_w, L0_attn_Wqkv_b, L0_attn_Wo_w,
           L0_attn_Wo_b, L0_attn_ghost, L0_ffn_ln_g, L0_ffn_ln_b,
           L0_ffn_w1, L0_ffn_w2):
    x = np.asarray(L0_attn_inputs, dtype=np.float32)
    wqkv = np.ascontiguousarray(np.asarray(L0_attn_Wqkv_w, np.float32).T)
    wo = np.ascontiguousarray(np.asarray(L0_attn_Wo_w, np.float32).T)
    w1 = np.ascontiguousarray(np.asarray(L0_ffn_w1, np.float32).T)
    w2 = np.ascontiguousarray(np.asarray(L0_ffn_w2, np.float32).T)
    qkvb_full = np.asarray(L0_attn_Wqkv_b, np.float32)
    qkvb = np.ascontiguousarray(qkvb_full[:2048].reshape(16, 128).T)
    vbias = np.ascontiguousarray(np.broadcast_to(qkvb_full[2048:], (128, 1024)))
    wob = np.ascontiguousarray(np.asarray(L0_attn_Wo_b, np.float32).reshape(8, 128).T)
    lng = np.ascontiguousarray(np.asarray(L0_ffn_ln_g, np.float32).reshape(8, 128).T)
    lnb = np.ascontiguousarray(np.asarray(L0_ffn_ln_b, np.float32).reshape(8, 128).T)
    ghost = np.ascontiguousarray(np.broadcast_to(
        np.asarray(L0_attn_ghost, np.float32)[None, :], (128, 16)))
    ident = np.eye(128, dtype=BF)
    qi = np.arange(128)[:, None]
    ki = np.arange(128)[None, :]
    mask = np.where(ki > qi, np.float32(NEG), np.float32(0.0)).astype(BF)
    bootf = np.geomspace(1.0 / 256.0, 1.0, NBOOT).astype(np.float32)[None, :]
    finef = np.linspace(0.5, 1.6, NFINE).astype(np.float32)[None, :]

    if "nc" not in _NC_CACHE:
        _NC_CACHE["nc"] = _build()
    nc = _NC_CACHE["nc"]

    in_maps = []
    for b in range(NCORE):
        in_maps.append({
            "x_t": np.ascontiguousarray(x[b].T),
            "wqkvT": wqkv, "woT": wo, "w1T": w1, "w2T": w2,
            "qkvb": qkvb, "vbiast": vbias, "wob": wob, "lng": lng, "lnb": lnb,
            "ghostt": ghost, "identb": ident, "maskb": mask,
            "bootf": bootf, "finef": finef,
        })
    res = run_bass_kernel_spmd(nc, in_maps, core_ids=list(range(NCORE)))
    out = np.stack([np.asarray(r["out_t"]).T for r in res.results], axis=0)
    return out.astype(np.float32)


# version 19 (final)
# speedup vs baseline: 1.1403x; 1.1403x over previous
import numpy as np
import ml_dtypes

import concourse.bass as bass
import concourse.mybir as mybir
from concourse.bass_utils import run_bass_kernel_spmd
from contextlib import ExitStack

F32 = mybir.dt.float32
BF16 = mybir.dt.bfloat16
AX = mybir.AxisListType.X
ALU = mybir.AluOpType
ACT = mybir.ActivationFunctionType
BF = ml_dtypes.bfloat16

NCORE = 8
D = 1024
T = 2048
H = 16
FF = 4096
NC = D // 128
NQG = T // 512
NM1 = FF // 128
EPS = 1e-5
MAGIC = 12582912.0
NEG = -1.0e30

N1 = NCORE * T * D
N2 = NCORE * T * FF
R1 = 0.005 * (N1 / 8.0)
R2 = 0.005 * (N2 / 32.0)
NBOOT = 6
NFINE = 10


def _build():
    nc = bass.Bass("TRN2", target_bir_lowering=False)

    x_d = nc.dram_tensor("x_t", [D, T], F32, kind="ExternalInput")
    wqkv_d = nc.dram_tensor("wqkvT", [D, 3 * D], F32, kind="ExternalInput")
    wo_d = nc.dram_tensor("woT", [D, D], F32, kind="ExternalInput")
    w1_d = nc.dram_tensor("w1T", [D, FF], F32, kind="ExternalInput")
    w2_d = nc.dram_tensor("w2T", [FF, D], F32, kind="ExternalInput")
    qkvb_d = nc.dram_tensor("qkvb", [128, 16], F32, kind="ExternalInput")
    vbias_d = nc.dram_tensor("vbiast", [128, D], F32, kind="ExternalInput")
    wob_d = nc.dram_tensor("wob", [128, NC], F32, kind="ExternalInput")
    lng_d = nc.dram_tensor("lng", [128, NC], F32, kind="ExternalInput")
    lnb_d = nc.dram_tensor("lnb", [128, NC], F32, kind="ExternalInput")
    ghost_d = nc.dram_tensor("ghostt", [128, H], F32, kind="ExternalInput")
    ident_d = nc.dram_tensor("identb", [128, 128], BF16, kind="ExternalInput")
    mask_d = nc.dram_tensor("maskb", [128, 128], BF16, kind="ExternalInput")
    bootf_d = nc.dram_tensor("bootf", [1, NBOOT], F32, kind="ExternalInput")
    finef_d = nc.dram_tensor("finef", [1, NFINE], F32, kind="ExternalInput")
    out_d = nc.dram_tensor("out_t", [D, T], F32, kind="ExternalOutput")

    h1_d = nc.dram_tensor("h1s", [D, T], F32, kind="Internal")
    x2_d = nc.dram_tensor("x2s", [D, T], F32, kind="Internal")
    h2_d = nc.dram_tensor("h2s", [FF, T], F32, kind="Internal")
    t1_d = nc.dram_tensor("t1s", [D, FF], BF16, kind="Internal")
    t2_d = nc.dram_tensor("t2s", [FF, D], BF16, kind="Internal")
    amx1_d = nc.dram_tensor("amx1", [128, 1], F32, kind="Internal")
    amx2_d = nc.dram_tensor("amx2", [128, 1], F32, kind="Internal")
    h1ss_d = nc.dram_tensor("h1ss", [128, 2048], F32, kind="Internal")
    ccd = {}
    for tag, n_in in [("a1", 2), ("b1", NBOOT), ("c1", NFINE),
                      ("a2", 2), ("b2", NBOOT), ("c2", NFINE)]:
        ccd[tag] = (nc.dram_tensor("cci_" + tag, [1, n_in], F32, kind="Internal"),
                    nc.dram_tensor("cco_" + tag, [1, n_in * NCORE], F32, kind="Internal"))

    EM = {"sync": [], "gpsimd": [], "tensor": [], "vector": [], "scalar": []}
    C = {}
    SEM = {}

    def emit(eng, fn):
        EM[eng].append(fn)

    ESEM = {"vector": "tv", "scalar": "ts"}
    DMAQ = {"lx"}

    def inc(eng, mk, semname, d, prewait=True):
        if prewait:
            s = ESEM.get(eng)
            if s is not None and C[s] > 0:
                sem0 = SEM[s]
                v0 = C[s]
                emit(eng, lambda e, sem0=sem0, v0=v0: e.wait_ge(sem0, v0))
        if semname in DMAQ and C[semname] > 0:
            semq = SEM[semname]
            vq = C[semname]
            emit(eng, lambda e, semq=semq, vq=vq: e.wait_ge(semq, vq))
        C[semname] += d
        sem = SEM[semname]
        emit(eng, lambda e, mk=mk, sem=sem, d=d: mk(e).then_inc(sem, d))
        return C[semname]

    def raw(eng, mk):
        s = ESEM.get(eng)
        if s is None:
            emit(eng, lambda e, mk=mk: mk(e))
            return
        wait(eng, s, C[s])
        inc(eng, mk, s, 1)

    def wait(eng, semname, val):
        if val <= 0:
            return
        sem = SEM[semname]
        emit(eng, lambda e, sem=sem, val=val: e.wait_ge(sem, val))

    def drain(eng):
        pass

    stack = ExitStack()
    with stack:
        E = stack.enter_context
        for s in ["lx", "lg", "cc", "tv", "ts", "tt", "so"]:
            SEM[s] = E(nc.semaphore(s))
            C[s] = 0

        qf = E(nc.sbuf_tensor("qf", [128, NC * T], BF16))
        kf = E(nc.sbuf_tensor("kf", [128, NC * T], BF16))
        vt = E(nc.sbuf_tensor("vt", [128, NC * T], BF16))
        qkvb = E(nc.sbuf_tensor("qkvbs", [128, 16], F32))
        vbias = E(nc.sbuf_tensor("vbiass", [128, D], BF16))
        wob = E(nc.sbuf_tensor("wobs", [128, NC], F32))
        lng = E(nc.sbuf_tensor("lngs", [128, NC], F32))
        lnb = E(nc.sbuf_tensor("lnbs", [128, NC], F32))
        ghost = E(nc.sbuf_tensor("ghosts", [128, H], F32))
        identb = E(nc.sbuf_tensor("identbs", [128, 128], BF16))
        maskb = E(nc.sbuf_tensor("maskbs", [128, 128], BF16))
        onesb = E(nc.sbuf_tensor("onesb", [128, 128], BF16))
        onesf = E(nc.sbuf_tensor("onesf", [128, 128], F32))
        onesrow = E(nc.sbuf_tensor("onesrow", [1, 128], F32))
        bootf = E(nc.sbuf_tensor("bootfs", [1, NBOOT], F32))
        finef = E(nc.sbuf_tensor("finefs", [1, NFINE], F32))
        cols = E(nc.sbuf_tensor("colsa", [128, 32], F32))
        rows = E(nc.sbuf_tensor("rowsa", [1, 128], F32))
        cnt1 = E(nc.sbuf_tensor("cnt1", [128, NFINE], F32))
        callc = E(nc.sbuf_tensor("callc", [128, 16 * NFINE], F32))
        cnt2 = cnt1
        bcnt = E(nc.sbuf_tensor("bcnt", [128, NBOOT], F32))
        thrb = E(nc.sbuf_tensor("thrb", [128, NBOOT], F32))
        thrf = E(nc.sbuf_tensor("thrf", [128, NFINE], F32))
        svals = E(nc.sbuf_tensor("svals", [128, 8], F32))
        grow = E(nc.sbuf_tensor("grow", [1, 32], F32))
        epscol = E(nc.sbuf_tensor("epscol", [128, 1], F32))

        ELL, EMX, RR, RINV = 0, 2, 4, 6
        CTMP = 20
        SV_S1, SV_GS, SV_S2, SV_EPS5 = 0, 1, 2, 3

        pbS = E(nc.psum_tensor("pbS", [128, 2048], F32))
        pbT = E(nc.psum_tensor("pbT", [128, 2048], BF16))
        pbA = E(nc.psum_tensor("pbA", [128, 512], F32))
        pbB = E(nc.psum_tensor("pbB", [128, 512], F32))
        pbY = pbB

        ln1x = E(nc.sbuf_tensor("ln1x", [128, NC * T], BF16))
        n1 = ln1x
        xg = E(nc.sbuf_tensor("xg", [128, 4096], F32))
        wgrp = E(nc.sbuf_tensor("wgrp", [128, 4096], BF16))
        pbuf = ln1x[:, 0:4096]
        ptb = ln1x[:, 4096:6144]
        asb = ln1x[:, 6144:10240]
        ysb = ln1x[:, 10240:14336]
        bscr = ln1x[:, 14336:15360]
        h1sb = E(nc.sbuf_tensor("h1sb", [128, 4096], F32))
        scrA = E(nc.sbuf_tensor("scrA", [128, 4096], F32))
        wstr = E(nc.sbuf_tensor("wstr", [128, 2048], F32))
        srep = E(nc.sbuf_tensor("srep", [128, 512 * 5], F32))

        def sr(i):
            if i == 5:
                return scrA[:, 3584:4096]
            return srep[:, 512 * i:512 * i + 512]

        inc("gpsimd", lambda e: e.dma_start(out=vbias[:], in_=vbias_d.ap()), "lg", 16)
        for dst, src in [(qkvb, qkvb_d), (wob, wob_d),
                         (lng, lng_d), (lnb, lnb_d), (ghost, ghost_d),
                         (identb, ident_d), (maskb, mask_d),
                         (bootf, bootf_d), (finef, finef_d)]:
            inc("sync", lambda e, dst=dst, src=src: e.dma_start(out=dst[:], in_=src.ap()), "lx", 16)
        for t_, v_ in [(onesb, 1.0), (onesf, 1.0), (onesrow, 1.0),
                       (cnt1, 0.0), (bcnt, 0.0), (epscol, EPS)]:
            raw("vector", lambda e, t_=t_, v_=v_: e.memset(t_[:], v_))
        raw("vector", lambda e: e.memset(cols[:], 0.0))
        inc("vector", lambda e: e.nop(), "tv", 1)

        for sweep in range(2):
            for c in range(NC):
                if sweep > 0 or c > 0:
                    wait("sync", "tv", C["tv"])
                lxv = inc("sync", lambda e, c=c: e.dma_start(
                    out=xg[:, 0:2048], in_=x_d.ap()[128 * c:128 * c + 128, :]), "lx", 16)
                wait("vector", "lx", lxv)
                if c > 0 or sweep > 0:
                    wait("vector", "tt", C["tt"])
                if sweep == 0:
                    tvv = inc("vector", lambda e: e.tensor_copy(
                        out=pbuf[:, 0:2048], in_=xg[:, 0:2048]), "tv", 1)
                    roff = 0
                else:
                    raw("vector", lambda e: e.tensor_copy(out=pbuf[:, 0:2048], in_=xg[:, 0:2048]))
                    tvv = inc("vector", lambda e: e.tensor_tensor(
                        out=pbuf[:, 2048:4096], in0=pbuf[:, 0:2048],
                        in1=pbuf[:, 0:2048], op=ALU.mult), "tv", 1)
                    roff = 2048
                wait("tensor", "tv", tvv)
                for qg in range(NQG):
                    inc("tensor", lambda e, c=c, qg=qg, roff=roff: e.matmul(
                        pbS[:, 512 * qg:512 * qg + 512], onesb[:],
                        pbuf[:, roff + 512 * qg:roff + 512 * qg + 512],
                        start=(c == 0), stop=(c == NC - 1)), "tt", 1)
            wait("vector", "tt", C["tt"])
            if sweep == 0:
                tvv = inc("vector", lambda e: e.tensor_scalar(
                    out=scrA[:, 0:2048], in0=pbS[:, 0:2048], scalar1=1.0 / D,
                    scalar2=None, op0=ALU.mult), "tv", 1)
                wait("tensor", "tv", tvv)
            else:
                raw("vector", lambda e: e.tensor_scalar(
                    out=scrA[:, 2048:4096], in0=pbS[:, 0:2048], scalar1=1.0 / D,
                    scalar2=None, op0=ALU.mult))
                drain("vector")
                raw("vector", lambda e: e.tensor_tensor(
                    out=wstr[:, 0:2048], in0=scrA[:, 0:2048], in1=scrA[:, 0:2048],
                    op=ALU.mult))
                drain("vector")
                tvv = inc("vector", lambda e: e.tensor_tensor(
                    out=scrA[:, 2048:4096], in0=scrA[:, 2048:4096],
                    in1=wstr[:, 0:2048], op=ALU.subtract), "tv", 1)
                wait("scalar", "tv", tvv)
                raw("scalar", lambda e: e.activation(
                    out=wstr[:, 0:2048], in_=scrA[:, 2048:4096], func=ACT.Ln, bias=epscol[:, 0:1]))
                tsv = inc("scalar", lambda e: e.activation(
                    out=wstr[:, 0:2048], in_=wstr[:, 0:2048], func=ACT.Exp,
                    scale=-0.5), "ts", 1)
                wait("vector", "ts", tsv)
                tvv = inc("vector", lambda e: e.scalar_tensor_tensor(
                    out=scrA[:, 0:2048], in0=scrA[:, 0:2048], scalar=-1.0,
                    in1=wstr[:, 0:2048], op0=ALU.mult, op1=ALU.mult), "tv", 1)
        for c in range(NC):
            wait("sync", "tv", C["tv"])
            lxv = inc("sync", lambda e, c=c: e.dma_start(
                out=xg[:, 0:2048], in_=x_d.ap()[128 * c:128 * c + 128, :]), "lx", 16)
            wait("vector", "lx", lxv)
            raw("vector", lambda e: e.tensor_tensor(
                out=xg[:, 2048:4096], in0=xg[:, 0:2048], in1=wstr[:, 0:2048], op=ALU.mult))
            drain("vector")
            tvv = inc("vector", lambda e, c=c: e.tensor_tensor(
                out=ln1x[:, 2048 * c:2048 * c + 2048], in0=xg[:, 2048:4096],
                in1=scrA[:, 0:2048], op=ALU.add), "tv", 1)
        tv_ln1 = C["tv"]

        wait("tensor", "tv", tv_ln1)

        for grp in range(4):
            m0 = 4 * grp
            wait("gpsimd", "tt", C["tt"])
            lgv = 0
            for k in range(NC):
                lgv = inc("gpsimd", lambda e, k=k, m0=m0: e.dma_start(
                    out=wgrp[:, 512 * k:512 * k + 512],
                    in_=wqkv_d.ap()[128 * k:128 * k + 128, 128 * m0:128 * m0 + 512]),
                    "lg", 16)
            wait("tensor", "lg", lgv)
            for qg in range(NQG):
                wait("tensor", "tv", C["tv"])
                for mm in range(4):
                    for k in range(NC):
                        inc("tensor", lambda e, qg=qg, mm=mm, k=k: e.matmul(
                            pbS[:, 512 * mm:512 * mm + 512],
                            wgrp[:, 512 * k + 128 * mm:512 * k + 128 * mm + 128],
                            ln1x[:, 2048 * k + 512 * qg:2048 * k + 512 * qg + 512],
                            start=(k == 0), stop=(k == NC - 1)), "tt", 1)
                ttv = C["tt"]
                wait("vector", "tt", ttv)
                for mm in range(4):
                    m = m0 + mm
                    is_q = m < 8
                    dst = qf if is_q else kf
                    mc = m if is_q else m - 8
                    inc("vector", lambda e, mm=mm, m=m, dst=dst, mc=mc, qg=qg,
                        is_q=is_q: e.tensor_scalar(
                            out=dst[:, 2048 * mc + 512 * qg:2048 * mc + 512 * qg + 512],
                            in0=pbS[:, 512 * mm:512 * mm + 512],
                            scalar1=qkvb[:, m:m + 1],
                            scalar2=(0.125 if is_q else None),
                            op0=ALU.add, op1=(ALU.mult if is_q else ALU.bypass)),
                        "tv", 1)
        for vg in range(2):
            wait("gpsimd", "tt", C["tt"])
            lgv = 0
            for k in range(NC):
                lgv = inc("gpsimd", lambda e, k=k, vg=vg: e.dma_start(
                    out=wgrp[:, 512 * k:512 * k + 512],
                    in_=wqkv_d.ap()[128 * k:128 * k + 128,
                                    2048 + 512 * vg:2048 + 512 * vg + 512]), "lg", 16)
            wait("tensor", "lg", lgv)
            if vg == 0:
                wait("vector", "lg", lgv)
            for tt_i in range(16):
                wait("tensor", "tv", C["tv"])
                for k in range(NC):
                    inc("tensor", lambda e, tt_i=tt_i, k=k: e.matmul(
                        pbS[:, 512 * (tt_i % 4):512 * (tt_i % 4) + 512],
                        ln1x[:, 2048 * k + 128 * tt_i:2048 * k + 128 * tt_i + 128],
                        wgrp[:, 512 * k:512 * k + 512],
                        start=(k == 0), stop=(k == NC - 1)), "tt", 1)
                ttv = C["tt"]
                wait("vector", "tt", ttv)
                raw("vector", lambda e, tt_i=tt_i, vg=vg: e.tensor_copy(
                    out=vt[:, 1024 * tt_i + 512 * vg:1024 * tt_i + 512 * vg + 512],
                    in_=pbS[:, 512 * (tt_i % 4):512 * (tt_i % 4) + 512]))
                drain("vector")
                inc("vector", lambda e, tt_i=tt_i, vg=vg: e.tensor_tensor(
                    out=vt[:, 1024 * tt_i + 512 * vg:1024 * tt_i + 512 * vg + 512],
                    in0=vt[:, 1024 * tt_i + 512 * vg:1024 * tt_i + 512 * vg + 512],
                    in1=vbias[:, 512 * vg:512 * vg + 512], op=ALU.add), "tv", 1)
        tv_qkv = C["tv"]
        wait("tensor", "tv", tv_qkv)

        tern_jobs = []
        for ch in range(NC):
            for half in range(2):
                tern_jobs.append(("g1", ch, half))
        for ch in range(0, 32, 2):
            tern_jobs.append(("g2", ch, 0))
        for ch in range(NC):
            for half in range(2):
                tern_jobs.append(("t1", ch, half))
        for ch in range(0, 32, 2):
            tern_jobs.append(("t2", ch, 0))
        tern_i = [0]
        tern_lx = [0]

        def w2row_ap(ch):
            return w2_d.ap()[128 * ch:128 * ch + 256, :].rearrange(
                "(a p) q -> p a q", p=128)

        def t2row_ap(ch):
            return t2_d.ap()[128 * ch:128 * ch + 256, :].rearrange(
                "(a p) q -> p a q", p=128)

        def v3d(ap2d):
            return ap2d.rearrange("p (a q) -> p a q", a=2)

        def emit_tern_some(n):
            for _ in range(n):
                if tern_i[0] >= len(tern_jobs):
                    return
                kind, ch, half = tern_jobs[tern_i[0]]
                tern_i[0] += 1
                wait("sync", "tv", C["tv"])
                if kind in ("g1", "t1"):
                    lxv = inc("sync", lambda e, ch=ch, half=half: e.dma_start(
                        out=wstr[:, 0:2048],
                        in_=w1_d.ap()[128 * ch:128 * ch + 128,
                                      2048 * half:2048 * half + 2048]), "lx", 16)
                else:
                    lxv = inc("sync", lambda e, ch=ch: e.dma_start(
                        out=v3d(wstr[:, 0:2048]), in_=w2row_ap(ch)), "lx", 16)
                if kind in ("g1", "g2"):
                    gc = 8 if kind == "g1" else 9
                    wait("scalar", "lx", lxv)
                    wait("scalar", "tv", C["tv"])
                    tsv = inc("scalar", lambda e: e.activation(
                        out=scrA[:, 0:2048], in_=wstr[:, 0:2048], func=ACT.Abs),
                        "ts", 1)
                    wait("vector", "ts", tsv)
                    raw("vector", lambda e: e.reduce_sum(
                        out=cols[:, CTMP:CTMP + 1], in_=scrA[:, 0:2048], axis=AX))
                    drain("vector")
                    tvv = inc("vector", lambda e, gc=gc: e.tensor_tensor(
                        out=cols[:, gc:gc + 1], in0=cols[:, gc:gc + 1],
                        in1=cols[:, CTMP:CTMP + 1], op=ALU.add), "tv", 1)
                    drain("vector")
                else:
                    igc = 10 if kind == "t1" else 11
                    wait("vector", "lx", lxv)
                    raw("vector", lambda e, igc=igc: e.tensor_scalar(
                        out=scrA[:, 0:2048], in0=wstr[:, 0:2048],
                        scalar1=cols[:, igc:igc + 1], scalar2=MAGIC,
                        op0=ALU.mult, op1=ALU.add))
                    drain("vector")
                    raw("vector", lambda e: e.tensor_scalar(
                        out=scrA[:, 0:2048], in0=scrA[:, 0:2048], scalar1=MAGIC,
                        scalar2=1.0, op0=ALU.subtract, op1=ALU.min))
                    drain("vector")
                    tvv = inc("vector", lambda e: e.tensor_scalar(
                        out=ptb[:, 0:2048], in0=scrA[:, 0:2048], scalar1=-1.0,
                        scalar2=None, op0=ALU.max), "tv", 1)
                    wait("sync", "tv", tvv)
                    if kind == "t1":
                        tern_lx[0] = inc("sync", lambda e, ch=ch, half=half: e.dma_start(
                            out=t1_d.ap()[128 * ch:128 * ch + 128,
                                          2048 * half:2048 * half + 2048],
                            in_=ptb[:, 0:2048]), "lx", 16)
                    else:
                        tern_lx[0] = inc("sync", lambda e, ch=ch: e.dma_start(
                            out=t2row_ap(ch), in_=v3d(ptb[:, 0:2048])), "lx", 16)

        def emit_gamma_finalize():
            wait("tensor", "tv", C["tv"])
            ttv = inc("tensor", lambda e: e.matmul(
                pbB[:, 0:2], onesf[:], cols[:, 8:10], start=True, stop=True), "tt", 1)
            wait("vector", "tt", ttv)
            for j in range(2):
                raw("vector", lambda e, j=j: e.tensor_scalar(
                    out=cols[:, 12 + j:13 + j], in0=pbB[:, j:j + 1],
                    scalar1=1.0 / (D * FF), scalar2=EPS, op0=ALU.mult, op1=ALU.add))
                drain("vector")
                raw("vector", lambda e, j=j: e.reciprocal(
                    out=cols[:, 10 + j:11 + j], in_=cols[:, 12 + j:13 + j]))
                drain("vector")
                raw("vector", lambda e, j=j: e.tensor_copy(
                    out=cols[:, 14 + j:15 + j], in_=cols[:, 12 + j:13 + j]))
                drain("vector")
            inc("vector", lambda e: e.nop(), "tv", 1)

        def stats_mm(rhs_tile, tvv_wait):
            wait("tensor", "tv", tvv_wait)
            for c in range(NC):
                inc("tensor", lambda e, c=c, rhs_tile=rhs_tile: e.matmul(
                    pbB[:, 0:512], onesb[:],
                    rhs_tile[:, 512 * c:512 * c + 512],
                    start=(c == 0), stop=(c == NC - 1)), "tt", 1)
            return C["tt"]

        def stats_sq_mm(src_ap, src_is_f32):
            for c in range(NC):
                sl = 512 * (c % 2)
                if c >= 2:
                    wait("vector", "tt", C["tt"])
                raw("vector", lambda e, c=c, sl=sl, src_ap=src_ap: e.tensor_tensor(
                    out=bscr[:, sl:sl + 512],
                    in0=src_ap[:, 512 * c:512 * c + 512],
                    in1=src_ap[:, 512 * c:512 * c + 512], op=ALU.mult))
                wait("tensor", "tv", C["tv"])
                inc("tensor", lambda e, c=c, sl=sl: e.matmul(
                    pbB[:, 0:512], onesb[:], bscr[:, sl:sl + 512],
                    start=(c == 0), stop=(c == NC - 1)), "tt", 1)
            return C["tt"]

        def stats_copy_mm(src_f32_ap):
            for c in range(NC):
                sl = 512 * (c % 2)
                if c >= 2:
                    wait("vector", "tt", C["tt"])
                raw("vector", lambda e, c=c, sl=sl, src_f32_ap=src_f32_ap:
                    e.tensor_copy(out=bscr[:, sl:sl + 512],
                                  in_=src_f32_ap[:, 512 * c:512 * c + 512]))
                wait("tensor", "tv", C["tv"])
                inc("tensor", lambda e, c=c, sl=sl: e.matmul(
                    pbB[:, 0:512], onesb[:], bscr[:, sl:sl + 512],
                    start=(c == 0), stop=(c == NC - 1)), "tt", 1)
            return C["tt"]

        tpP = [0, 0]
        sc_free = [0, 0, 0, 0]
        q1ss_lx = [0]

        for qg in range(NQG):
            wait("sync", "tv", C["tv"])
            lx_x = inc("sync", lambda e, qg=qg: e.dma_start(
                out=xg[:].rearrange("p (c q) -> p c q", c=NC),
                in_=x_d.ap().rearrange("(c p) q -> p c q", p=128)[
                    :, :, 512 * qg:512 * qg + 512]), "lx", 16)

            if tern_lx[0]:
                wait("vector", "lx", tern_lx[0])
            iters = [(h, qi) for h in range(H) for qi in range(4)]
            exp_of = {}
            pn_tv = {}

            def emit_scores_exp(it, qg=qg, iters=iters, exp_of=exp_of):
                h, qi = iters[it]
                ip = it % 2
                hb = 64 * (h % 2)
                hc = h // 2
                qt = 4 * qg + qi
                kext = 128 * (qt + 1)
                widths = [512] * (kext // 512)
                if kext % 512:
                    widths.append(kext % 512)
                piece_ts = [0] * len(widths)
                pend = []
                tsv = 0
                for p, w in enumerate(widths):
                    bk = 2 * ip + (p % 2)
                    if p >= 2:
                        wait("tensor", "ts", piece_ts[p - 2])
                    elif sc_free[bk]:
                        wait("tensor", "ts", sc_free[bk])
                    last = (p == len(widths) - 1)
                    inc("tensor", lambda e, hb=hb, hc=hc, qt=qt, p=p, w=w, bk=bk,
                        last=last: e.matmul(
                        pbS[:, 512 * bk:512 * bk + w],
                        qf[hb:hb + 64, 2048 * hc + 128 * qt:2048 * hc + 128 * qt + 128],
                        kf[hb:hb + 64, 2048 * hc + 512 * p:2048 * hc + 512 * p + w],
                        start=True, stop=not last), "tt", 1)
                    if last:
                        inc("tensor", lambda e, bk=bk, w=w: e.matmul(
                            pbS[:, 512 * bk + w - 128:512 * bk + w], identb[:],
                            maskb[:], start=False, stop=True), "tt", 1)
                    pend.append((p, w))
                    if p % 2 == 1 or last:
                        ttv = C["tt"]
                        wait("scalar", "tt", ttv)
                        j = pend[0][0] // 2
                        if j == 0:
                            wait("scalar", "tt", tpP[ip])
                        we = sum(w_ for _, w_ in pend)
                        acc_col = (ELL if j == 0 else RR) + ip
                        tsv = inc("scalar", lambda e, ip=ip, j=j, we=we,
                                  acc_col=acc_col: e.activation(
                            out=pbuf[:, 2048 * ip + 1024 * j:
                                     2048 * ip + 1024 * j + we],
                            in_=pbS[:, 1024 * ip:1024 * ip + we], func=ACT.Exp,
                            accum_out=cols[:, acc_col:acc_col + 1]), "ts", 1)
                        if j > 0:
                            tsv = inc("scalar", lambda e, ip=ip: e.add(
                                out=cols[:, ELL + ip:ELL + ip + 1],
                                in_=cols[:, ELL + ip:ELL + ip + 1],
                                add=cols[:, RR + ip:RR + ip + 1]), "ts", 1)
                        for p_, _ in pend:
                            piece_ts[p_] = tsv
                            sc_free[2 * ip + (p_ % 2)] = tsv
                        pend = []
                exp_of[it] = tsv

            def emit_vec(it, qg=qg, iters=iters, exp_of=exp_of, pn_tv=pn_tv):
                h, qi = iters[it]
                ip = it % 2
                qt = 4 * qg + qi
                kext = 128 * (qt + 1)
                wait("vector", "ts", exp_of[it])
                raw("vector", lambda e, ip=ip, kext=kext: e.reduce_max(
                    out=cols[:, EMX + ip:EMX + ip + 1],
                    in_=pbuf[:, 2048 * ip:2048 * ip + kext], axis=AX))
                if ip == 1:
                    raw("vector", lambda e, h=h: e.scalar_tensor_tensor(
                        out=cols[:, RR:RR + 2],
                        in0=cols[:, EMX:EMX + 2],
                        scalar=ghost[:, h:h + 1],
                        in1=cols[:, ELL:ELL + 2],
                        op0=ALU.mult, op1=ALU.add))
                    raw("vector", lambda e: e.reciprocal(
                        out=cols[:, RINV:RINV + 2],
                        in_=cols[:, RR:RR + 2]))
                    kxe = kext - 128
                    raw("vector", lambda e, kxe=kxe: e.tensor_scalar(
                        out=pbuf[:, 0:kxe],
                        in0=pbuf[:, 0:kxe],
                        scalar1=cols[:, RINV:RINV + 1], scalar2=None,
                        op0=ALU.mult))
                    pn_tv[it - 1] = C["tv"]
                    pn_tv[it] = inc("vector", lambda e, kext=kext:
                                    e.tensor_scalar(
                        out=pbuf[:, 2048:2048 + kext],
                        in0=pbuf[:, 2048:2048 + kext],
                        scalar1=cols[:, RINV + 1:RINV + 2], scalar2=None,
                        op0=ALU.mult), "tv", 1)

            ev_ts = {}
            ptb_ev = {}
            pb_i = [0]

            def emit_transposes(it, qg=qg, iters=iters, pn_tv=pn_tv):
                h, qi = iters[it]
                ip = it % 2
                qt = 4 * qg + qi
                nkb = qt + 1
                wait("tensor", "tv", pn_tv[it])
                esc = False
                for b0 in range(0, nkb, 8):
                    bn = min(8, nkb - b0)
                    bi = pb_i[0]
                    pb_i[0] += 1
                    slot = 1024 * (bi % 2)
                    g = ptb_ev.get(bi - 2)
                    if g:
                        wait("tensor", g[0], g[1])
                    for b in range(bn):
                        inc("tensor", lambda e, ip=ip, b0=b0, b=b, slot=slot:
                            e.transpose(
                            pbT[:, slot + 128 * b:slot + 128 * b + 128],
                            pbuf[:, 2048 * ip + 128 * (b0 + b):
                                 2048 * ip + 128 * (b0 + b) + 128],
                            identb[:]), "tt", 1)
                    ttv = C["tt"]
                    if esc:
                        wait("scalar", "tt", ttv)
                        v = inc("scalar", lambda e, b0=b0, bn=bn, slot=slot:
                                e.activation(
                            out=ptb[:, 128 * b0:128 * b0 + 128 * bn],
                            in_=pbT[:, slot:slot + 128 * bn],
                            func=ACT.Copy), "ts", 1)
                        ptb_ev[bi] = ("ts", v)
                    else:
                        wait("vector", "tt", ttv)
                        v = inc("vector", lambda e, b0=b0, bn=bn, slot=slot:
                                e.tensor_copy(
                            out=ptb[:, 128 * b0:128 * b0 + 128 * bn],
                            in_=pbT[:, slot:slot + 128 * bn]), "tv", 1)
                        ptb_ev[bi] = ("tv", v)
                tpP[ip] = C["tt"]
                ev_ts[it] = ("ts", C["ts"]) if esc else ("tv", C["tv"])

            def emit_pv(it, qg=qg, iters=iters):
                h, qi = iters[it]
                qt = 4 * qg + qi
                nkb = qt + 1
                wait("tensor", ev_ts[it][0], ev_ts[it][1])
                for kb in range(nkb):
                    inc("tensor", lambda e, kb=kb, h=h, qi=qi, nkb=nkb: e.matmul(
                        pbA[0:64, 128 * qi:128 * qi + 128],
                        vt[:, 1024 * kb + 64 * h:1024 * kb + 64 * h + 64],
                        ptb[:, 128 * kb:128 * kb + 128],
                        start=(kb == 0),
                        stop=(kb == nkb - 1)), "tt", 1)
                if qi == 3:
                    ttv = C["tt"]
                    wait("vector", "tt", ttv)
                    tsa = inc("vector", lambda e, h=h: e.tensor_copy(
                        out=asb[64 * (h % 2):64 * (h % 2) + 64,
                                512 * (h // 2):512 * (h // 2) + 512],
                        in_=pbA[0:64, 0:512]), "tv", 1)
                    wait("tensor", "tv", tsa)

            emit_scores_exp(0)
            emit_vec(0)
            for it in range(64):
                if it > 0:
                    emit_transposes(it - 1)
                if it + 1 < 64:
                    emit_scores_exp(it + 1)
                if it > 0:
                    emit_pv(it - 1)
                if it + 1 < 64:
                    emit_vec(it + 1)
            emit_transposes(63)
            emit_pv(63)
            for mh in range(2):
                wait("gpsimd", "tt", C["tt"])
                lgv = 0
                for c in range(NC):
                    lgv = inc("gpsimd", lambda e, c=c, mh=mh: e.dma_start(
                        out=wgrp[:, 512 * c:512 * c + 512],
                        in_=wo_d.ap()[128 * c:128 * c + 128,
                                      512 * mh:512 * mh + 512]), "lg", 16)
                wait("tensor", "lg", lgv)
                for mm in range(4):
                    m = 4 * mh + mm
                    for c in range(NC):
                        inc("tensor", lambda e, m=m, mm=mm, c=c: e.matmul(
                            pbY[:, 0:512],
                            wgrp[:, 512 * c + 128 * mm:512 * c + 128 * mm + 128],
                            asb[:, 512 * c:512 * c + 512],
                            start=(c == 0), stop=(c == NC - 1)), "tt", 1)
                    ttv = C["tt"]
                    wait("vector", "tt", ttv)
                    tvy = inc("vector", lambda e, m=m: e.tensor_scalar(
                        out=ysb[:, 512 * m:512 * m + 512], in0=pbY[:, 0:512],
                        scalar1=wob[:, m:m + 1], scalar2=None, op0=ALU.add), "tv", 1)
                    wait("tensor", "tv", tvy)
            tvv = inc("vector", lambda e: e.nop(), "tv", 1)
            ttv = stats_mm(ysb, tvv)
            wait("vector", "tt", ttv)
            tvv = inc("vector", lambda e: e.tensor_scalar(
                out=sr(0), in0=pbB[:, 0:512], scalar1=1.0 / D, scalar2=None,
                op0=ALU.mult), "tv", 1)
            wait("tensor", "tv", tvv)
            ttv = stats_sq_mm(ysb, False)
            wait("vector", "tt", ttv)
            raw("vector", lambda e: e.tensor_scalar(
                out=sr(2), in0=pbB[:, 0:512], scalar1=1.0 / D, scalar2=None,
                op0=ALU.mult))
            drain("vector")
            raw("vector", lambda e: e.tensor_tensor(
                out=sr(5), in0=sr(0), in1=sr(0), op=ALU.mult))
            drain("vector")
            tvv = inc("vector", lambda e: e.tensor_tensor(
                out=sr(2), in0=sr(2), in1=sr(5), op=ALU.subtract), "tv", 1)
            wait("scalar", "tv", tvv)
            raw("scalar", lambda e: e.activation(out=sr(1), in_=sr(2), func=ACT.Ln, bias=epscol[:, 0:1]))
            tsv = inc("scalar", lambda e: e.activation(
                out=sr(1), in_=sr(1), func=ACT.Exp, scale=-0.5), "ts", 1)
            wait("vector", "ts", tsv)
            tvv = inc("vector", lambda e: e.scalar_tensor_tensor(
                out=sr(0), in0=sr(0), scalar=-1.0, in1=sr(1),
                op0=ALU.mult, op1=ALU.mult), "tv", 1)
            wait("vector", "lx", lx_x)
            for c in range(NC):
                raw("vector", lambda e, c=c: e.tensor_tensor(
                    out=scrA[:, 0:512], in0=ysb[:, 512 * c:512 * c + 512],
                    in1=sr(1), op=ALU.mult))
                drain("vector")
                raw("vector", lambda e: e.tensor_tensor(
                    out=scrA[:, 0:512], in0=scrA[:, 0:512], in1=sr(0), op=ALU.add))
                drain("vector")
                raw("vector", lambda e, c=c: e.tensor_tensor(
                    out=xg[:, 512 * c:512 * c + 512],
                    in0=xg[:, 512 * c:512 * c + 512],
                    in1=scrA[:, 0:512], op=ALU.add))
                drain("vector")
            tv_x2 = inc("vector", lambda e: e.nop(), "tv", 1)
            wait("sync", "tv", tv_x2)
            inc("sync", lambda e, qg=qg: e.dma_start(
                out=x2_d.ap().rearrange("(c p) q -> p c q", p=128)[
                    :, :, 512 * qg:512 * qg + 512],
                in_=xg[:].rearrange("p (c q) -> p c q", c=NC)), "lx", 16)
            ttv = stats_copy_mm(xg)
            wait("vector", "tt", ttv)
            tvv = inc("vector", lambda e: e.tensor_scalar(
                out=sr(0), in0=pbB[:, 0:512], scalar1=1.0 / D, scalar2=None,
                op0=ALU.mult), "tv", 1)
            wait("tensor", "tv", tvv)
            ttv = stats_sq_mm(xg, True)
            wait("vector", "tt", ttv)
            raw("vector", lambda e: e.tensor_scalar(
                out=sr(2), in0=pbB[:, 0:512], scalar1=1.0 / D, scalar2=None,
                op0=ALU.mult))
            drain("vector")
            raw("vector", lambda e: e.tensor_tensor(
                out=sr(5), in0=sr(0), in1=sr(0), op=ALU.mult))
            drain("vector")
            tvv = inc("vector", lambda e: e.tensor_tensor(
                out=sr(2), in0=sr(2), in1=sr(5), op=ALU.subtract), "tv", 1)
            wait("scalar", "tv", tvv)
            raw("scalar", lambda e: e.activation(out=sr(1), in_=sr(2), func=ACT.Ln, bias=epscol[:, 0:1]))
            tsv = inc("scalar", lambda e: e.activation(
                out=sr(1), in_=sr(1), func=ACT.Exp, scale=-0.5), "ts", 1)
            wait("vector", "ts", tsv)
            raw("vector", lambda e: e.tensor_tensor(
                out=sr(5), in0=sr(1), in1=sr(1), op=ALU.mult))
            drain("vector")
            tvv = inc("vector", lambda e: e.tensor_tensor(
                out=sr(5), in0=sr(5), in1=sr(2), op=ALU.mult), "tv", 1)
            wait("scalar", "tv", tvv)
            raw("scalar", lambda e: e.activation(out=sr(3), in_=sr(5), func=ACT.Ln, bias=epscol[:, 0:1]))
            tsv = inc("scalar", lambda e: e.activation(
                out=sr(3), in_=sr(3), func=ACT.Exp, scale=-0.5), "ts", 1)
            wait("vector", "ts", tsv)
            raw("vector", lambda e: e.tensor_tensor(
                out=sr(3), in0=sr(3), in1=sr(1), op=ALU.mult))
            drain("vector")
            tvv = inc("vector", lambda e: e.scalar_tensor_tensor(
                out=sr(4), in0=sr(0), scalar=-1.0, in1=sr(3),
                op0=ALU.mult, op1=ALU.mult), "tv", 1)
            for c in range(NC):
                raw("vector", lambda e, c=c: e.tensor_scalar(
                    out=sr(5), in0=sr(3), scalar1=lng[:, c:c + 1],
                    scalar2=None, op0=ALU.mult))
                drain("vector")
                raw("vector", lambda e, c=c: e.tensor_tensor(
                    out=scrA[:, 0:512], in0=xg[:, 512 * c:512 * c + 512],
                    in1=sr(5), op=ALU.mult))
                drain("vector")
                raw("vector", lambda e, c=c: e.tensor_scalar(
                    out=sr(5), in0=sr(4), scalar1=lng[:, c:c + 1],
                    scalar2=lnb[:, c:c + 1], op0=ALU.mult, op1=ALU.add))
                drain("vector")
                raw("vector", lambda e, c=c: e.tensor_tensor(
                    out=h1sb[:, 512 * c:512 * c + 512], in0=scrA[:, 0:512],
                    in1=sr(5), op=ALU.add))
                drain("vector")
            raw("vector", lambda e: e.reduce_max(
                out=cols[:, CTMP:CTMP + 1], in_=h1sb[:, 0:4096:4], axis=AX,
                apply_absolute_value=True))
            raw("vector", lambda e: e.tensor_tensor(
                out=cols[:, 16:17], in0=cols[:, 16:17],
                in1=cols[:, CTMP:CTMP + 1], op=ALU.max))
            if q1ss_lx[0]:
                wait("vector", "lx", q1ss_lx[0])
            for c in range(NC):
                raw("vector", lambda e, c=c: e.tensor_copy(
                    out=scrA[:, 2048 + 64 * c:2048 + 64 * c + 64],
                    in_=h1sb[:, 512 * c:512 * c + 512:8]))
            tv_h1 = inc("vector", lambda e: e.nop(), "tv", 1)
            wait("sync", "tv", tv_h1)
            inc("sync", lambda e, qg=qg: e.dma_start(
                out=h1_d.ap().rearrange("(c p) q -> p c q", p=128)[
                    :, :, 512 * qg:512 * qg + 512],
                in_=h1sb[:].rearrange("p (c q) -> p c q", c=NC)), "lx", 16)
            q1ss_lx[0] = inc("sync", lambda e, qg=qg: e.dma_start(
                out=h1ss_d.ap()[:, 512 * qg:512 * qg + 512],
                in_=scrA[:, 2048:2560]), "lx", 16)
            if qg == 0:
                emit_tern_some(24)
            elif qg == 1:
                emit_tern_some(8)
                emit_gamma_finalize()
                emit_tern_some(8)
            else:
                emit_tern_some(24)
        emit_tern_some(len(tern_jobs))

        def part_reduce_max_col(col_ap, bounce_d):
            tvv = inc("vector", lambda e: e.nop(), "tv", 1)
            wait("gpsimd", "tv", tvv)
            lgv = inc("gpsimd", lambda e, col_ap=col_ap, bounce_d=bounce_d:
                      e.dma_start(out=bounce_d.ap(), in_=col_ap), "lg", 16)
            wait("gpsimd", "lg", lgv)
            lgv = inc("gpsimd", lambda e, bounce_d=bounce_d: e.dma_start(
                out=wstr[0:1, 1024:1152],
                in_=bounce_d.ap().rearrange("(a p) x -> a (p x)", a=1)), "lg", 16)
            wait("vector", "lg", lgv)
            raw("vector", lambda e: e.reduce_max(out=rows[0:1, 0:1], in_=wstr[0:1, 1024:1152], axis=AX))
            drain("vector")

        def allgather_row(src_off, nvals, tag):
            tvv = inc("vector", lambda e: e.nop(), "tv", 1)
            wait("gpsimd", "tv", tvv)
            lgv = inc("gpsimd", lambda e, src_off=src_off, nvals=nvals, tag=tag:
                      e.dma_start(out=ccd[tag][0].ap(),
                                  in_=rows[0:1, src_off:src_off + nvals]), "lg", 16)
            wait("gpsimd", "lg", lgv)
            ccv = inc("gpsimd", lambda e, tag=tag: e.collective_compute(
                "AllGather", ALU.bypass, replica_groups=[list(range(NCORE))],
                ins=[ccd[tag][0].ap()], outs=[ccd[tag][1].ap()]), "cc", 1)
            wait("gpsimd", "cc", ccv)
            lgv = inc("gpsimd", lambda e, tag=tag, nvals=nvals: e.dma_start(
                out=wstr[0:1, 0:NCORE * nvals], in_=ccd[tag][1].ap()), "lg", 16)
            wait("vector", "lg", lgv)

        def bcast_row_to_tile(src_off, nvals, dst_tile):
            tvv = inc("vector", lambda e: e.nop(), "tv", 1)
            wait("tensor", "tv", tvv)
            ttv = inc("tensor", lambda e, src_off=src_off, nvals=nvals: e.matmul(
                pbB[:, 0:nvals], onesrow[:], rows[0:1, src_off:src_off + nvals],
                start=True, stop=True), "tt", 1)
            wait("vector", "tt", ttv)
            return inc("vector", lambda e, dst_tile=dst_tile, nvals=nvals:
                       e.tensor_copy(out=dst_tile[:, 0:nvals], in_=pbB[:, 0:nvals]),
                       "tv", 1)

        def count_resident(tile, dummy, thr_tile, cnt_tile, nthr):
            for j in range(nthr):
                raw("vector", lambda e, j=j, tile=tile, dummy=dummy,
                    thr_tile=thr_tile, cnt_tile=cnt_tile: e.tensor_scalar(
                    out=dummy, in0=tile,
                    scalar1=thr_tile[:, j:j + 1], scalar2=None,
                    op0=ALU.is_gt, op1=ALU.add,
                    accum_out=cnt_tile[:, j:j + 1]))
            return inc("vector", lambda e: e.nop(), "tv", 1)

        def count_finish(cnt_tile, nthr, tag, r_target):
            tvv = inc("vector", lambda e: e.nop(), "tv", 1)
            wait("tensor", "tv", tvv)
            ttv = inc("tensor", lambda e, cnt_tile=cnt_tile, nthr=nthr: e.matmul(
                pbB[:, 0:nthr], onesf[:], cnt_tile[:, 0:nthr],
                start=True, stop=True), "tt", 1)
            wait("vector", "tt", ttv)
            tvv = inc("vector", lambda e, nthr=nthr: e.tensor_copy(
                out=rows[0:1, 32:32 + nthr], in_=pbB[0:1, 0:nthr]), "tv", 1)
            allgather_row(32, nthr, tag)
            raw("vector", lambda e, nthr=nthr: e.tensor_copy(
                out=rows[0:1, 32:32 + nthr], in_=wstr[0:1, 0:nthr]))
            drain("vector")
            for cc_i in range(1, NCORE):
                raw("vector", lambda e, cc_i=cc_i, nthr=nthr: e.tensor_tensor(
                    out=rows[0:1, 32:32 + nthr], in0=rows[0:1, 32:32 + nthr],
                    in1=wstr[0:1, nthr * cc_i:nthr * cc_i + nthr], op=ALU.add))
                drain("vector")
            raw("vector", lambda e, nthr=nthr, r_target=r_target: e.tensor_scalar(
                out=rows[0:1, 64:64 + nthr], in0=rows[0:1, 32:32 + nthr],
                scalar1=float(r_target), scalar2=None, op0=ALU.is_gt))
            drain("vector")
            raw("vector", lambda e: e.memset(rows[0:1, 64:65], 1.0))
            drain("vector")
            raw("vector", lambda e, nthr=nthr: e.memset(rows[0:1, 64 + nthr - 1:64 + nthr], 0.0))
            drain("vector")
            raw("vector", lambda e, nthr=nthr: e.tensor_tensor(
                out=rows[0:1, 80:80 + nthr - 1], in0=rows[0:1, 64:64 + nthr - 1],
                in1=rows[0:1, 65:64 + nthr], op=ALU.subtract))
            drain("vector")

            def msum(dst, src, shift, nthr=nthr):
                raw("vector", lambda e, dst=dst, src=src, shift=shift: e.tensor_tensor(
                    out=rows[0:1, 96:96 + nthr - 1],
                    in0=rows[0:1, src + shift:src + shift + nthr - 1],
                    in1=rows[0:1, 80:80 + nthr - 1], op=ALU.mult))
                drain("vector")
                raw("vector", lambda e, dst=dst: e.reduce_sum(
                    out=rows[0:1, dst:dst + 1], in_=rows[0:1, 96:96 + nthr - 1],
                    axis=AX))
                drain("vector")
            msum(2, 48, 0)
            msum(3, 48, 1)
            msum(4, 32, 0)
            msum(5, 32, 1)
            raw("vector", lambda e: e.tensor_tensor(
                out=rows[0:1, 6:7], in0=rows[0:1, 4:5], in1=rows[0:1, 5:6],
                op=ALU.subtract))
            drain("vector")
            raw("vector", lambda e: e.tensor_scalar(
                out=rows[0:1, 6:7], in0=rows[0:1, 6:7], scalar1=1e-9,
                scalar2=None, op0=ALU.max))
            drain("vector")
            raw("vector", lambda e: e.reciprocal(out=rows[0:1, 7:8], in_=rows[0:1, 6:7]))
            drain("vector")
            raw("vector", lambda e, r_target=r_target: e.tensor_scalar(
                out=rows[0:1, 8:9], in0=rows[0:1, 4:5], scalar1=float(-r_target),
                scalar2=None, op0=ALU.add))
            drain("vector")
            raw("vector", lambda e: e.tensor_tensor(
                out=rows[0:1, 8:9], in0=rows[0:1, 8:9], in1=rows[0:1, 7:8],
                op=ALU.mult))
            drain("vector")
            raw("vector", lambda e: e.tensor_tensor(
                out=rows[0:1, 9:10], in0=rows[0:1, 3:4], in1=rows[0:1, 2:3],
                op=ALU.subtract))
            drain("vector")
            raw("vector", lambda e: e.tensor_tensor(
                out=rows[0:1, 9:10], in0=rows[0:1, 9:10], in1=rows[0:1, 8:9],
                op=ALU.mult))
            drain("vector")
            raw("vector", lambda e: e.tensor_tensor(
                out=rows[0:1, 0:1], in0=rows[0:1, 2:3], in1=rows[0:1, 9:10],
                op=ALU.add))
            drain("vector")
            return inc("vector", lambda e: e.nop(), "tv", 1)

        def emit_quantile(which):
            acol = 16 if which == 1 else 17
            bounce = amx1_d if which == 1 else amx2_d
            tags = ("a1", "b1", "c1") if which == 1 else ("a2", "b2", "c2")
            r_t = R1 if which == 1 else R2
            cnt_b, cnt_f = bcnt, (cnt1 if which == 1 else cnt2)
            tile = scrA[:, 0:2048] if which == 1 else scrA[:, 2048:4096]
            dummy = h1sb[:, 0:2048]
            wait("vector", "lx", C["lx"])
            if which == 1:
                wait("sync", "tv", C["tv"])
                lxv = inc("sync", lambda e: e.dma_start(
                    out=scrA[:, 0:2048], in_=h1ss_d.ap()), "lx", 16)
                wait("scalar", "lx", lxv)
            wait("scalar", "tv", C["tv"])
            tsv = inc("scalar", lambda e, tile=tile: e.activation(
                out=tile, in_=tile, func=ACT.Abs), "ts", 1)
            wait("vector", "ts", tsv)

            def count_pass(thr_tile, cnt_tile, nthr, tile=tile, dummy=dummy):
                return count_resident(tile, dummy, thr_tile, cnt_tile, nthr)
            part_reduce_max_col(cols[:, acol:acol + 1], bounce)
            raw("vector", lambda e: e.memset(rows[0:1, 1:2], 0.0))
            drain("vector")
            allgather_row(0, 2, tags[0])
            raw("vector", lambda e: e.reduce_max(
                out=rows[0:1, 0:1], in_=wstr[0:1, 0:2 * NCORE], axis=AX))
            drain("vector")
            raw("vector", lambda e: e.tensor_scalar(
                out=rows[0:1, 48:48 + NBOOT], in0=bootf[:], scalar1=rows[0:1, 0:1],
                scalar2=None, op0=ALU.mult))
            drain("vector")
            raw("vector", lambda e: e.tensor_copy(
                out=grow[0:1, 0:NBOOT], in_=rows[0:1, 48:48 + NBOOT]))
            drain("vector")
            raw("vector", lambda e, cnt_b=cnt_b: e.memset(cnt_b[:], 0.0))
            drain("vector")
            bcast_row_to_tile(48, NBOOT, thrb)
            count_pass(thrb, cnt_b, NBOOT)
            raw("vector", lambda e: e.tensor_copy(
                out=rows[0:1, 48:48 + NBOOT], in_=grow[0:1, 0:NBOOT]))
            drain("vector")
            count_finish(cnt_b, NBOOT, tags[1], r_t)
            raw("vector", lambda e: e.tensor_scalar(
                out=rows[0:1, 48:48 + NFINE], in0=finef[:], scalar1=rows[0:1, 0:1],
                scalar2=None, op0=ALU.mult))
            drain("vector")
            raw("vector", lambda e: e.tensor_copy(
                out=grow[0:1, 8:8 + NFINE], in_=rows[0:1, 48:48 + NFINE]))
            drain("vector")
            raw("vector", lambda e, cnt_f=cnt_f: e.memset(cnt_f[:], 0.0))
            drain("vector")
            bcast_row_to_tile(48, NFINE, thrf)
            count_pass(thrf, cnt_f, NFINE)
            raw("vector", lambda e: e.tensor_copy(
                out=rows[0:1, 48:48 + NFINE], in_=grow[0:1, 8:8 + NFINE]))
            drain("vector")
            count_finish(cnt_f, NFINE, tags[2], r_t)
            raw("vector", lambda e: e.tensor_scalar(
                out=rows[0:1, 1:2], in0=rows[0:1, 0:1], scalar1=1e-5,
                scalar2=None, op0=ALU.max))
            drain("vector")
            raw("vector", lambda e: e.reciprocal(out=rows[0:1, 2:3], in_=rows[0:1, 1:2]))
            drain("vector")
            raw("vector", lambda e: e.tensor_scalar(
                out=rows[0:1, 2:3], in0=rows[0:1, 2:3], scalar1=127.0,
                scalar2=None, op0=ALU.mult))
            drain("vector")
            return _bcast_sv()

        def _bcast_sv():
            tvv = inc("vector", lambda e: e.nop(), "tv", 1)
            wait("tensor", "tv", tvv)
            ttv = inc("tensor", lambda e: e.matmul(
                pbB[:, 0:3], onesrow[:], rows[0:1, 0:3], start=True, stop=True),
                "tt", 1)
            wait("vector", "tt", ttv)
            return inc("vector", lambda e: e.tensor_copy(
                out=svals[:, 4:7], in_=pbB[:, 0:3]), "tv", 1)

        emit_quantile(1)
        drain("vector")
        raw("vector", lambda e: e.tensor_copy(
            out=svals[:, SV_S1:SV_S1 + 1], in_=svals[:, 6:7]))
        drain("vector")
        raw("vector", lambda e: e.tensor_tensor(
            out=svals[:, SV_GS:SV_GS + 1], in0=svals[:, 5:6], in1=cols[:, 14:15],
            op=ALU.mult))
        drain("vector")
        tvv = inc("vector", lambda e: e.tensor_scalar(
            out=svals[:, SV_GS:SV_GS + 1], in0=svals[:, SV_GS:SV_GS + 1],
            scalar1=1.0 / 127.0, scalar2=None, op0=ALU.mult), "tv", 1)
        q_lx = {}
        for c in range(NC + 1):
            if c < NC:
                if c >= 2 or c == 0:
                    wait("sync", "tv", C["tv"])
                q_lx[c] = inc("sync", lambda e, c=c: e.dma_start(
                    out=(scrA[:, 0:2048] if c % 2 == 0 else h1sb[:, 0:2048]),
                    in_=h1_d.ap()[128 * c:128 * c + 128, :]), "lx", 16)
            cj = c - 1
            if 0 <= cj < NC:
                buf = scrA[:, 0:2048] if cj % 2 == 0 else h1sb[:, 0:2048]
                wait("vector", "lx", q_lx[cj])
                raw("vector", lambda e, buf=buf: e.tensor_scalar(
                    out=wstr[:, 0:2048], in0=buf,
                    scalar1=svals[:, SV_S1:SV_S1 + 1], scalar2=MAGIC,
                    op0=ALU.mult, op1=ALU.add))
                raw("vector", lambda e: e.tensor_scalar(
                    out=wstr[:, 0:2048], in0=wstr[:, 0:2048], scalar1=MAGIC,
                    scalar2=127.0, op0=ALU.subtract, op1=ALU.min))
                raw("vector", lambda e, cj=cj: e.tensor_scalar(
                    out=n1[:, 2048 * cj:2048 * cj + 2048], in0=wstr[:, 0:2048],
                    scalar1=-128.0, scalar2=None, op0=ALU.max))
        tv_n1 = inc("vector", lambda e: e.nop(), "tv", 1)

        t1a, t1b = qf, kf
        wait("gpsimd", "tt", C["tt"])
        wait("gpsimd", "lx", C["lx"])
        for ch in range(NC):
            dst = t1a if ch < 4 else t1b
            off = 4096 * (ch % 4)
            inc("gpsimd", lambda e, ch=ch, dst=dst, off=off: e.dma_start(
                out=dst[:, off:off + 4096],
                in_=t1_d.ap()[128 * ch:128 * ch + 128, :]), "lg", 16)
        lg_t1 = C["lg"]
        wait("tensor", "lg", lg_t1)
        wait("tensor", "tv", tv_n1)
        wait("tensor", "ts", C["ts"])
        wait("scalar", "tv", tv_n1)
        h2dma = [0, 0, 0, 0]
        gelu_ts = [0, 0, 0, 0]
        sub_tv = [0, 0, 0, 0]
        for qg4 in range(NQG):
            for m in range(NM1):
                ms = m % 4
                if gelu_ts[ms]:
                    wait("tensor", "ts", gelu_ts[ms])
                for k in range(NC):
                    src = t1a if k < 4 else t1b
                    off = 4096 * (k % 4)
                    inc("tensor", lambda e, m=m, ms=ms, k=k, qg4=qg4, src=src,
                        off=off: e.matmul(
                        pbS[:, 512 * ms:512 * ms + 512],
                        src[:, off + 128 * m:off + 128 * m + 128],
                        n1[:, 2048 * k + 512 * qg4:2048 * k + 512 * qg4 + 512],
                        start=(k == 0), stop=(k == NC - 1)), "tt", 1)
                ttv = C["tt"]
                wait("scalar", "tt", ttv)
                if h2dma[ms]:
                    wait("scalar", "lx", h2dma[ms])
                if sub_tv[ms]:
                    wait("scalar", "tv", sub_tv[ms])
                gelu_ts[ms] = inc("scalar", lambda e, ms=ms: e.activation(
                    out=scrA[:, 512 * ms:512 * ms + 512],
                    in_=pbS[:, 512 * ms:512 * ms + 512],
                    func=ACT.Gelu, scale=svals[:, SV_GS:SV_GS + 1]), "ts", 1)
                wait("sync", "ts", gelu_ts[ms])
                h2dma[ms] = inc("sync", lambda e, m=m, qg4=qg4, ms=ms: e.dma_start(
                    out=h2_d.ap()[128 * m:128 * m + 128, 512 * qg4:512 * qg4 + 512],
                    in_=scrA[:, 512 * ms:512 * ms + 512]), "lx", 16)
                wait("vector", "ts", gelu_ts[ms])
                raw("vector", lambda e, ms=ms, m=m, qg4=qg4: e.tensor_copy(
                    out=scrA[:, 2048 + 64 * m + 16 * qg4:
                             2048 + 64 * m + 16 * qg4 + 16],
                    in_=scrA[:, 512 * ms:512 * ms + 512:32]))
                if m % 4 == 0:
                    raw("vector", lambda e, ms=ms: e.reduce_max(
                        out=cols[:, CTMP:CTMP + 1],
                        in_=scrA[:, 512 * ms:512 * ms + 512:4], axis=AX,
                        apply_absolute_value=True))
                    raw("vector", lambda e: e.tensor_tensor(
                        out=cols[:, 17:18], in0=cols[:, 17:18],
                        in1=cols[:, CTMP:CTMP + 1], op=ALU.max))
                sub_tv[ms] = C["tv"]

        wait("vector", "lx", C["lx"])
        emit_quantile(2)
        drain("vector")
        raw("vector", lambda e: e.tensor_copy(
            out=svals[:, SV_S2:SV_S2 + 1], in_=svals[:, 6:7]))
        drain("vector")
        raw("vector", lambda e: e.tensor_tensor(
            out=cols[:, 18:19], in0=svals[:, 5:6], in1=cols[:, 15:16], op=ALU.mult))
        drain("vector")
        raw("vector", lambda e: e.tensor_scalar(
            out=cols[:, 18:19], in0=cols[:, 18:19], scalar1=1.0 / 127.0,
            scalar2=None, op0=ALU.mult))
        drain("vector")
        raw("vector", lambda e: e.tensor_tensor(
            out=cols[:, 19:20], in0=cols[:, 18:19], in1=cols[:, 18:19], op=ALU.mult))
        drain("vector")
        raw("vector", lambda e: e.reciprocal(out=cols[:, 19:20], in_=cols[:, 19:20]))
        drain("vector")
        tvv = inc("vector", lambda e: e.tensor_scalar(
            out=svals[:, SV_EPS5:SV_EPS5 + 1], in0=cols[:, 19:20], scalar1=EPS,
            scalar2=None, op0=ALU.mult), "tv", 1)
        tv_s2 = C["tv"]

        wait("gpsimd", "tt", C["tt"])
        wait("gpsimd", "lx", C["lx"])
        for ch in range(32):
            dst = t1a if ch < 16 else t1b
            off = 1024 * (ch % 16)
            inc("gpsimd", lambda e, ch=ch, dst=dst, off=off: e.dma_start(
                out=dst[:, off:off + 1024],
                in_=t2_d.ap()[128 * ch:128 * ch + 128, :]), "lg", 16)
        lg_t2 = C["lg"]
        wait("tensor", "lg", lg_t2)
        wait("tensor", "tv", tv_s2)
        n2 = vt
        q_tv = {}
        pre_lx = {}

        def emit_half_load(qg5, hi):
            quarter, hh = hi // 2, hi % 2
            sl = 2048 * (hi % 2)
            return inc("sync", lambda e, qg5=qg5, quarter=quarter, hh=hh,
                       sl=sl: e.dma_start(
                out=scrA[:, sl:sl + 2048].rearrange("p (a q) -> p a q", a=4),
                in_=h2_d.ap().rearrange("(a p) q -> p a q", p=128)[
                    :, 8 * quarter + 4 * hh:8 * quarter + 4 * hh + 4,
                    512 * qg5:512 * qg5 + 512]), "lx", 16)

        for qg5 in range(NQG):
            for hi in range(8):
                sl = 2048 * (hi % 2)
                if (qg5, hi) in pre_lx:
                    lxv = pre_lx.pop((qg5, hi))
                else:
                    if hi >= 2:
                        wait("sync", "tv", q_tv[hi - 2])
                    else:
                        wait("sync", "tv", C["tv"])
                    lxv = emit_half_load(qg5, hi)
                wait("vector", "lx", lxv)
                raw("vector", lambda e, sl=sl: e.tensor_scalar(
                    out=wstr[:, 0:2048], in0=scrA[:, sl:sl + 2048],
                    scalar1=svals[:, SV_S2:SV_S2 + 1], scalar2=MAGIC,
                    op0=ALU.mult, op1=ALU.add))
                raw("vector", lambda e: e.tensor_scalar(
                    out=wstr[:, 0:2048], in0=wstr[:, 0:2048], scalar1=MAGIC,
                    scalar2=127.0, op0=ALU.subtract, op1=ALU.min))
                q_tv[hi] = inc("vector", lambda e, hi=hi: e.tensor_scalar(
                    out=n2[:, 2048 * hi:2048 * hi + 2048],
                    in0=wstr[:, 0:2048], scalar1=-128.0, scalar2=None,
                    op0=ALU.max), "tv", 1)
            u_ts = [0, 0, 0, 0]
            u_tv = [0, 0, 0, 0]
            for mg in range(2):
                for hi in range(8):
                    wait("tensor", "tv", q_tv[hi])
                    for mi in range(4):
                        m = 4 * mg + mi
                        ms = mi
                        if hi == 0:
                            if u_ts[ms]:
                                wait("tensor", "ts", u_ts[ms])
                            if u_tv[ms]:
                                wait("tensor", "tv", u_tv[ms])
                        for k in range(4 * hi, 4 * hi + 4):
                            src = t1a if k < 16 else t1b
                            off = 1024 * (k % 16)
                            inc("tensor", lambda e, m=m, ms=ms, k=k, src=src,
                                off=off: e.matmul(
                                pbS[:, 512 * ms:512 * ms + 512],
                                src[:, off + 128 * m:off + 128 * m + 128],
                                n2[:, 512 * k:512 * k + 512],
                                start=(k == 0), stop=(k == 31)), "tt", 1)
                        if hi == 7:
                            ttv = C["tt"]
                            wait("scalar", "tt", ttv)
                            u_ts[ms] = inc("scalar", lambda e, m=m, ms=ms:
                                           e.activation(
                                out=h1sb[:, 512 * m:512 * m + 512],
                                in_=pbS[:, 512 * ms:512 * ms + 512],
                                func=ACT.Copy), "ts", 1)
                            wait("vector", "ts", u_ts[ms])
                            u_tv[ms] = inc("vector", lambda e, m=m:
                                           e.tensor_copy(
                                out=ysb[:, 512 * m:512 * m + 512],
                                in_=h1sb[:, 512 * m:512 * m + 512]), "tv", 1)
            wait("vector", "ts", C["ts"])
            tvv = inc("vector", lambda e: e.nop(), "tv", 1)
            ttv = stats_mm(ysb, tvv)
            wait("vector", "tt", ttv)
            tvv = inc("vector", lambda e: e.tensor_scalar(
                out=sr(0), in0=pbB[:, 0:512], scalar1=1.0 / D, scalar2=None,
                op0=ALU.mult), "tv", 1)
            wait("tensor", "tv", tvv)
            ttv = stats_sq_mm(ysb, False)
            wait("vector", "tt", ttv)
            raw("vector", lambda e: e.tensor_scalar(
                out=sr(2), in0=pbB[:, 0:512], scalar1=1.0 / D, scalar2=None,
                op0=ALU.mult))
            drain("vector")
            raw("vector", lambda e: e.tensor_tensor(
                out=sr(5), in0=sr(0), in1=sr(0), op=ALU.mult))
            drain("vector")
            raw("vector", lambda e: e.tensor_tensor(
                out=sr(2), in0=sr(2), in1=sr(5), op=ALU.subtract))
            drain("vector")
            tvv = inc("vector", lambda e: e.tensor_scalar(
                out=sr(2), in0=sr(2), scalar1=svals[:, SV_EPS5:SV_EPS5 + 1],
                scalar2=None, op0=ALU.add), "tv", 1)
            wait("scalar", "tv", tvv)
            raw("scalar", lambda e: e.activation(out=sr(1), in_=sr(2), func=ACT.Ln, bias=0.0))
            tsv = inc("scalar", lambda e: e.activation(
                out=sr(1), in_=sr(1), func=ACT.Exp, scale=-0.5), "ts", 1)
            wait("vector", "ts", tsv)
            tvv = inc("vector", lambda e: e.scalar_tensor_tensor(
                out=sr(4), in0=sr(0), scalar=-1.0, in1=sr(1),
                op0=ALU.mult, op1=ALU.mult), "tv", 1)
            if qg5 + 1 < NQG:
                wait("sync", "tv", q_tv[6])
                pre_lx[(qg5 + 1, 0)] = emit_half_load(qg5 + 1, 0)
            wait("sync", "tv", tvv)
            wait("sync", "so", C["so"])
            lx_x2 = inc("sync", lambda e, qg5=qg5: e.dma_start(
                out=xg[:].rearrange("p (c q) -> p c q", c=NC),
                in_=x2_d.ap().rearrange("(c p) q -> p c q", p=128)[
                    :, :, 512 * qg5:512 * qg5 + 512]), "lx", 16)
            wait("vector", "lx", lx_x2)
            for c in range(NC):
                raw("vector", lambda e, c=c: e.tensor_tensor(
                    out=wstr[:, 0:512], in0=h1sb[:, 512 * c:512 * c + 512],
                    in1=sr(1), op=ALU.mult))
                drain("vector")
                raw("vector", lambda e: e.tensor_tensor(
                    out=wstr[:, 0:512], in0=wstr[:, 0:512], in1=sr(4), op=ALU.add))
                drain("vector")
                raw("vector", lambda e, c=c: e.tensor_tensor(
                    out=xg[:, 512 * c:512 * c + 512],
                    in0=xg[:, 512 * c:512 * c + 512],
                    in1=wstr[:, 0:512], op=ALU.add))
                drain("vector")
            tv_out = inc("vector", lambda e: e.nop(), "tv", 1)
            wait("sync", "tv", tv_out)
            inc("sync", lambda e, qg5=qg5: e.dma_start(
                out=out_d.ap().rearrange("(c p) q -> p c q", p=128)[
                    :, :, 512 * qg5:512 * qg5 + 512],
                in_=xg[:].rearrange("p (c q) -> p c q", c=NC)), "so", 16)
        wait("sync", "so", C["so"])

        block = E(nc.Block())

        @block.sync
        def _(e):
            for f in EM["sync"]:
                f(e)

        @block.gpsimd
        def _(e):
            for f in EM["gpsimd"]:
                f(e)

        @block.tensor
        def _(e):
            for f in EM["tensor"]:
                f(e)

        @block.vector
        def _(e):
            for f in EM["vector"]:
                f(e)

        @block.scalar
        def _(e):
            for f in EM["scalar"]:
                f(e)

    return nc


_NC_CACHE = {}


def kernel(L0_attn_inputs, L0_attn_Wqkv_w, L0_attn_Wqkv_b, L0_attn_Wo_w,
           L0_attn_Wo_b, L0_attn_ghost, L0_ffn_ln_g, L0_ffn_ln_b,
           L0_ffn_w1, L0_ffn_w2):
    x = np.asarray(L0_attn_inputs, dtype=np.float32)
    wqkv = np.ascontiguousarray(np.asarray(L0_attn_Wqkv_w, np.float32).T)
    wo = np.ascontiguousarray(np.asarray(L0_attn_Wo_w, np.float32).T)
    w1 = np.ascontiguousarray(np.asarray(L0_ffn_w1, np.float32).T)
    w2 = np.ascontiguousarray(np.asarray(L0_ffn_w2, np.float32).T)
    qkvb_full = np.asarray(L0_attn_Wqkv_b, np.float32)
    qkvb = np.ascontiguousarray(qkvb_full[:2048].reshape(16, 128).T)
    vbias = np.ascontiguousarray(np.broadcast_to(qkvb_full[2048:], (128, 1024)))
    wob = np.ascontiguousarray(np.asarray(L0_attn_Wo_b, np.float32).reshape(8, 128).T)
    lng = np.ascontiguousarray(np.asarray(L0_ffn_ln_g, np.float32).reshape(8, 128).T)
    lnb = np.ascontiguousarray(np.asarray(L0_ffn_ln_b, np.float32).reshape(8, 128).T)
    ghost = np.ascontiguousarray(np.broadcast_to(
        np.asarray(L0_attn_ghost, np.float32)[None, :], (128, 16)))
    ident = np.eye(128, dtype=BF)
    qi = np.arange(128)[:, None]
    ki = np.arange(128)[None, :]
    mask = np.where(ki > qi, np.float32(NEG), np.float32(0.0)).astype(BF)
    bootf = np.geomspace(1.0 / 256.0, 1.0, NBOOT).astype(np.float32)[None, :]
    finef = np.linspace(0.5, 1.6, NFINE).astype(np.float32)[None, :]

    if "nc" not in _NC_CACHE:
        _NC_CACHE["nc"] = _build()
    nc = _NC_CACHE["nc"]

    in_maps = []
    for b in range(NCORE):
        in_maps.append({
            "x_t": np.ascontiguousarray(x[b].T),
            "wqkvT": wqkv, "woT": wo, "w1T": w1, "w2T": w2,
            "qkvb": qkvb, "vbiast": vbias, "wob": wob, "lng": lng, "lnb": lnb,
            "ghostt": ghost, "identb": ident, "maskb": mask,
            "bootf": bootf, "finef": finef,
        })
    res = run_bass_kernel_spmd(nc, in_maps, core_ids=list(range(NCORE)))
    out = np.stack([np.asarray(r["out_t"]).T for r in res.results], axis=0)
    return out.astype(np.float32)
